# version 08
# baseline (speedup 1.0000x reference)
import os
os.environ.setdefault("TRN_ALLOW_ICI_ALL_TO_ALL", "1")

import numpy as np
import concourse.bass as bass
import concourse.mybir as mybir
from concourse.bass_utils import run_bass_kernel_spmd

F32 = mybir.dt.float32
BF16 = mybir.dt.bfloat16
AF = mybir.ActivationFunctionType
ALU = mybir.AluOpType

D = 2048
S = 8192
NH = 16
HD = 128
NCORE = 8
HPC = 2
TC = 1024
INTER = 8192
KCH = D // 128
TCHUNK = 512
NTCH = S // TCHUNK
NTB = S // 512
QSCALE = 1.0 / float(np.sqrt(HD))
EPS = 1e-5
PART = 128



class _RecInst:
    __slots__ = ("incs",)

    def __init__(self):
        self.incs = []

    def then_inc(self, sem, n=None):
        self.incs.append((sem, n))
        return self


class _RecEngine:
    def __init__(self, log):
        self._log = log

    def __getattr__(self, meth):
        log = self._log

        def call(*a, **k):
            rec = _RecInst()
            log.append((meth, a, k, rec))
            return rec
        return call


class RecBlock:

    def __init__(self):
        self.logs = {e: [] for e in
                     ("tensor", "scalar", "vector", "gpsimd", "sync")}

    def _dec(self, engine):
        def deco(fn):
            fn(_RecEngine(self.logs[engine]))
        return deco

    @property
    def tensor(self):
        return self._dec("tensor")

    @property
    def scalar(self):
        return self._dec("scalar")

    @property
    def vector(self):
        return self._dec("vector")

    @property
    def gpsimd(self):
        return self._dec("gpsimd")

    @property
    def sync(self):
        return self._dec("sync")

    def replay(self, nc):
        with nc.Block() as blk:
            for engine in ("tensor", "scalar", "vector", "gpsimd", "sync"):
                log = self.logs[engine]

                def emit(eng, log=log):
                    for meth, a, k, rec in log:
                        inst = getattr(eng, meth)(*a, **k)
                        for sem, n in rec.incs:
                            if n is None:
                                inst.then_inc(sem)
                            else:
                                inst.then_inc(sem, n)
                getattr(blk, engine)(emit)


class Ctr:
    def __init__(self, nc, name):
        self.sem = nc.alloc_semaphore(name)
        self.v = 0

    def inc(self, inst, n=1):
        inst.then_inc(self.sem, n)
        self.v += n
        return self.v

    def inc_default(self, inst):
        inst.then_inc(self.sem)
        self.v = max(self.v, 1)
        return 1


def build_nc(phases=("p1", "p2", "p3", "p4", "p5", "p6")):
    nc = bass.Bass("TRN2", target_bir_lowering=False, debug=False)

    xT = nc.dram_tensor("xT", [D, S], F32, kind="ExternalInput").ap()
    wq = nc.dram_tensor("wq", [D, HPC * HD], F32, kind="ExternalInput").ap()
    wk = nc.dram_tensor("wk", [D, HPC * HD], F32, kind="ExternalInput").ap()
    wv = nc.dram_tensor("wv", [D, HPC * HD], F32, kind="ExternalInput").ap()
    bq = nc.dram_tensor("bq", [HPC * HD], F32, kind="ExternalInput").ap()
    bk = nc.dram_tensor("bk", [HPC * HD], F32, kind="ExternalInput").ap()
    bv = nc.dram_tensor("bv", [HPC * HD], F32, kind="ExternalInput").ap()
    wo = nc.dram_tensor("wo", [D, D], F32, kind="ExternalInput").ap()
    bo = nc.dram_tensor("bo", [D], F32, kind="ExternalInput").ap()
    xres = nc.dram_tensor("xres", [D, TC], F32, kind="ExternalInput").ap()
    wgate = nc.dram_tensor("wgate", [D, INTER], F32, kind="ExternalInput").ap()
    wup = nc.dram_tensor("wup", [D, INTER], F32, kind="ExternalInput").ap()
    wdown = nc.dram_tensor("wdown", [INTER, D], F32, kind="ExternalInput").ap()
    y_out = nc.dram_tensor("y", [TC, D], F32, kind="ExternalOutput").ap()

    cc_in = nc.dram_tensor("cc_in", [D, TC], BF16)
    cc_out = nc.dram_tensor("cc_out", [D, TC], BF16)
    h_dram = nc.dram_tensor("h_dram", [TC, INTER], BF16)
    x1_dram = nc.dram_tensor("x1_dram", [D, TC], F32)
    wo_bf = nc.dram_tensor("wo_bf", [D, D], BF16)
    wd_bf = nc.dram_tensor("wd_bf", [INTER, D], BF16)

    AR_BYTES = 152 * 1024
    arena = nc.alloc_sbuf_tensor("arena", [PART, AR_BYTES // 2], BF16)
    ar_bf = arena.ap()
    ar_f32 = arena.ap().bitcast(F32)

    def carve_bf(off_kb, cols):
        c0 = off_kb * 1024 // 2
        return ar_bf[:, c0:c0 + cols]

    def carve_f32(off_kb, cols):
        c0 = off_kb * 1024 // 4
        return ar_f32[:, c0:c0 + cols]

    q_sb = carve_bf(0, HPC * S)
    k_sb = carve_bf(32, HPC * S)
    v_sb = carve_bf(64, 64 * 256)
    xbuf = [carve_bf(96, KCH * TCHUNK), carve_bf(112, KCH * TCHUNK)]
    wq_sb = carve_bf(128, KCH * 256)
    wk_sb = carve_bf(136, KCH * 256)
    wv_sb = carve_bf(144, KCH * 256)
    aoh = [carve_bf(0, KCH * 512), carve_bf(16, KCH * 512)]
    wo_buf = [carve_bf(48, KCH * 128), carve_bf(52, KCH * 128)]
    z_sb = [carve_f32(56, KCH * 512), carve_f32(88, KCH * 512)]
    x1bf = carve_bf(120, KCH * TC)
    wg_buf = [carve_bf(0, KCH * 512), carve_bf(16, KCH * 512)]
    wu_buf = [carve_bf(32, KCH * 512), carve_bf(48, KCH * 512)]
    hfm_q = [carve_bf(0, 16 * 1024), carve_bf(32, 16 * 1024)]
    wd_buf = [carve_bf(64, 16 * 128), carve_bf(68, 16 * 128)]
    z2_sb = carve_f32(80, KCH * TC)
    panel = [carve_f32(0, D), carve_f32(8, D)]

    ones_col_b = nc.alloc_sbuf_tensor("ones_col_b", [PART, 1], BF16)
    ones_row_b = nc.alloc_sbuf_tensor("ones_row_b", [1, PART], BF16)
    ones_row_f = nc.alloc_sbuf_tensor("ones_row_f", [1, PART], F32)
    identf = nc.alloc_sbuf_tensor("identf", [PART, PART], F32)
    causal_b = nc.alloc_sbuf_tensor("causal_b", [PART, PART], BF16)
    epst = nc.alloc_sbuf_tensor("epst", [1, 1], F32)
    bq_sb = nc.alloc_sbuf_tensor("bq_sb", [PART, HPC], F32)
    bk_sb = nc.alloc_sbuf_tensor("bk_sb", [PART, HPC], F32)
    bv_row = nc.alloc_sbuf_tensor("bv_row", [1, HPC * HD], BF16)
    bo_sb = nc.alloc_sbuf_tensor("bo_sb", [PART, KCH], F32)

    p_sb = nc.alloc_sbuf_tensor("p_sb", [PART, 2 * 1024], BF16)
    ao_sb = nc.alloc_sbuf_tensor("ao_sb", [PART, 2 * 512], BF16)
    dinv_sb = nc.alloc_sbuf_tensor("dinv_sb", [1, 512], F32)
    bc_sb = nc.alloc_sbuf_tensor("bc_sb", [PART, 512], F32)

    mean_sb = nc.alloc_sbuf_tensor("mean_sb", [1, 512], F32)
    msq_sb = nc.alloc_sbuf_tensor("msq_sb", [1, 512], F32)
    tmp_row = nc.alloc_sbuf_tensor("tmp_row", [1, 512], F32)
    ln_row = nc.alloc_sbuf_tensor("ln_row", [1, 512], F32)
    rstd_row = nc.alloc_sbuf_tensor("rstd_row", [1, 512], F32)
    meanbc = nc.alloc_sbuf_tensor("meanbc", [PART, 512], F32)
    rstdbc = nc.alloc_sbuf_tensor("rstdbc", [PART, 512], F32)
    zbf_t = nc.alloc_sbuf_tensor("zbf_t", [PART, 2 * 512], BF16)
    sq_t = nc.alloc_sbuf_tensor("sq_t", [PART, 2 * 512], BF16)

    sg_t = nc.alloc_sbuf_tensor("sg_t", [PART, 2 * 512], F32)
    h_t = nc.alloc_sbuf_tensor("h_t", [PART, 512], F32)
    A_t = nc.alloc_sbuf_tensor("A_t", [PART, 512], F32)
    pM01 = nc.alloc_sbuf_tensor("pM01", [PART, 128], F32)
    pm01 = nc.alloc_sbuf_tensor("pm01", [PART, 128], F32)
    pM23 = nc.alloc_sbuf_tensor("pM23", [PART, 128], F32)
    pm23 = nc.alloc_sbuf_tensor("pm23", [PART, 128], F32)
    thr_t = nc.alloc_sbuf_tensor("thr_t", [PART, 128], F32)
    mask_t = nc.alloc_sbuf_tensor("mask_t", [PART, 512], BF16)
    hsp_t = nc.alloc_sbuf_tensor("hsp_t", [PART, 2 * 512], BF16)
    outt = nc.alloc_sbuf_tensor("outt", [PART, 2 * 128], F32)

    wide0 = nc.alloc_psum_tensor("wide0", [PART, 1024], F32)
    wide1 = nc.alloc_psum_tensor("wide1", [PART, 1024], F32)
    pbA = nc.alloc_psum_tensor("pbA", [PART, 512], F32)
    pbB = nc.alloc_psum_tensor("pbB", [PART, 512], F32)
    pbC = nc.alloc_psum_tensor("pbC", [PART, 512], F32)
    pbD = nc.alloc_psum_tensor("pbD", [PART, 512], F32)
    wides = [wide0, wide1]
    pAB = [pbA, pbB]

    s_gld = Ctr(nc, "s_gld")
    s_xb = [Ctr(nc, "s_xb0"), Ctr(nc, "s_xb1")]
    s_wo = [Ctr(nc, "s_wo0"), Ctr(nc, "s_wo1")]
    s_wgu = [Ctr(nc, "s_wgu0"), Ctr(nc, "s_wgu1")]
    s_wd = [Ctr(nc, "s_wd0"), Ctr(nc, "s_wd1")]
    s_sld = Ctr(nc, "s_sld")
    s_hT = [Ctr(nc, "s_hT0"), Ctr(nc, "s_hT1")]
    s_ao = [Ctr(nc, "s_ao0"), Ctr(nc, "s_ao1")]
    s_x1 = [Ctr(nc, "s_x1a"), Ctr(nc, "s_x1b")]
    s_hs = [Ctr(nc, "s_hs0"), Ctr(nc, "s_hs1")]
    s_ot = [Ctr(nc, "s_ot0"), Ctr(nc, "s_ot1")]
    s_gc = Ctr(nc, "s_gc")
    s_pe = Ctr(nc, "s_pe")
    s_ac = Ctr(nc, "s_ac")
    s_dv = Ctr(nc, "s_dv")
    s_cc = Ctr(nc, "s_cc")

    R = {}

    if True:
        block = RecBlock()
        @block.gpsimd
        def _(g):
            g.memset(ones_col_b[:, :], 1.0)
            g.memset(ones_row_b[:, :], 1.0)
            g.memset(ones_row_f[:, :], 1.0)
            g.memset(epst[:, :], EPS)
            g.memset(identf[:, :], 1.0)
            ist = g.memset(causal_b[:, :], 1.0)
            s_gc.inc(ist)
            g.wait_ge(s_gc.sem, s_gc.v)
            g.affine_select(identf[:, :], identf[:, :], [[1, PART]], ALU.is_equal,
                            0.0, base=0, channel_multiplier=-1)
            ist = g.affine_select(causal_b[:, :], causal_b[:, :], [[1, PART]], ALU.is_ge,
                                  0.0, base=0, channel_multiplier=-1)
            s_gc.inc(ist)
            s_gld.inc(g.dma_start(out=wq_sb.rearrange("p (c j) -> p c j", j=256),
                                  in_=wq.rearrange("(c p) j -> p c j", p=PART)), 16)
            s_gld.inc(g.dma_start(out=wk_sb.rearrange("p (c j) -> p c j", j=256),
                                  in_=wk.rearrange("(c p) j -> p c j", p=PART)), 16)
            s_gld.inc(g.dma_start(out=wv_sb.rearrange("p (c j) -> p c j", j=256),
                                  in_=wv.rearrange("(c p) j -> p c j", p=PART)), 16)
            for h in range(HPC):
                s_gld.inc(g.dma_start(out=bq_sb[:, h], in_=bq[h * HD:(h + 1) * HD]), 16)
                s_gld.inc(g.dma_start(out=bk_sb[:, h], in_=bk[h * HD:(h + 1) * HD]), 16)
            s_gld.inc(g.dma_start(out=bv_row[0:1, :],
                                  in_=bv.rearrange("(a j) -> a j", a=1)), 16)
            for jc in range(KCH):
                s_gld.inc(g.dma_start(out=bo_sb[:, jc],
                                      in_=bo[jc * PART:(jc + 1) * PART]), 16)
        R["consts_gc"] = s_gc.v
        R["consts_gld"] = s_gld.v

        if "p1" in phases:
            for tc in range(NTCH):
                par = tc % 2

                @block.gpsimd
                def _(g, tc=tc, par=par):
                    if tc >= 2:
                        g.wait_ge(s_pe.sem, R[("xbuf_free", tc - 2)])
                    ist = g.dma_start(
                        out=xbuf[par].rearrange("p (c t) -> p c t", t=TCHUNK),
                        in_=xT[:, tc * TCHUNK:(tc + 1) * TCHUNK]
                            .rearrange("(c p) t -> p c t", p=PART))
                    s_xb[par].inc(ist, 16)
                    R[("xbuf", tc)] = s_xb[par].v

                @block.tensor
                def _(t, tc=tc, par=par):
                    if tc == 0:
                        t.wait_ge(s_gc.sem, R["consts_gc"])
                        t.wait_ge(s_gld.sem, R["consts_gld"])
                    t.wait_ge(s_xb[par].sem, R[("xbuf", tc)])
                    if tc >= 1:
                        t.wait_ge(s_ac.sem, R[("exp_last", 1, tc - 1)])
                    for h in range(HPC):
                        for kc in range(KCH):
                            ist = t.matmul(wide0[:, h * 512:(h + 1) * 512],
                                           wq_sb[:, (kc * 2 + h) * HD:(kc * 2 + h + 1) * HD],
                                           xbuf[par][:, kc * TCHUNK:(kc + 1) * TCHUNK],
                                           start=(kc == 0), stop=(kc == KCH - 1))
                        R[("q_done", tc, h)] = s_pe.inc(ist)
                        for kc in range(KCH):
                            ist = t.matmul(wide1[:, h * 512:(h + 1) * 512],
                                           wk_sb[:, (kc * 2 + h) * HD:(kc * 2 + h + 1) * HD],
                                           xbuf[par][:, kc * TCHUNK:(kc + 1) * TCHUNK],
                                           start=(kc == 0), stop=(kc == KCH - 1))
                        R[("k_done", tc, h)] = s_pe.inc(ist)
                for tb in range(4):
                    @block.tensor
                    def _(t, tc=tc, par=par, tb=tb):
                        vp = pAB[tb % 2]
                        if tb >= 2:
                            t.wait_ge(s_dv.sem, R[("v_evac_tb", tc, tb - 2)])
                        elif tc >= 1:
                            t.wait_ge(s_dv.sem, R[("v_evac_tb", tc - 1, tb + 2)])
                            t.wait_ge(s_dv.sem, R[("ao", tb % 2, tc - 1)])
                        for kc in range(KCH):
                            t.matmul(vp[:, 0:256],
                                     xbuf[par][:, kc * TCHUNK + tb * PART:
                                               kc * TCHUNK + (tb + 1) * PART],
                                     wv_sb[:, kc * 256:(kc + 1) * 256],
                                     start=(kc == 0), stop=False)
                        ist = t.matmul(vp[:, 0:256], ones_row_b[0:1, :],
                                       bv_row[0:1, :], start=False, stop=True)
                        R[("v_done", tc, tb)] = s_pe.inc(ist)
                        if tb == 3:
                            R[("xbuf_free", tc)] = s_pe.v

                    @block.vector
                    def _(v, tc=tc, tb=tb):
                        vp = pAB[tb % 2]
                        v.wait_ge(s_pe.sem, R[("v_done", tc, tb)])
                        tbg = tc * 4 + tb
                        ist = v.tensor_copy(v_sb[:, tbg * 256:(tbg + 1) * 256],
                                            vp[:, 0:256])
                        R[("v_evac_tb", tc, tb)] = s_dv.inc(ist)

                @block.scalar
                def _(sc, tc=tc):
                    for h in range(HPC):
                        sc.wait_ge(s_pe.sem, R[("q_done", tc, h)])
                        ist = sc.activation(
                            q_sb[:, h * S + tc * TCHUNK: h * S + (tc + 1) * TCHUNK],
                            wide0[:, h * 512:(h + 1) * 512], AF.Identity,
                            bias=bq_sb[:, h:h + 1])
                        s_ac.inc(ist)
                        sc.wait_ge(s_pe.sem, R[("k_done", tc, h)])
                        ist = sc.activation(
                            k_sb[:, h * S + tc * TCHUNK: h * S + (tc + 1) * TCHUNK],
                            wide1[:, h * 512:(h + 1) * 512], AF.Identity,
                            bias=bk_sb[:, h:h + 1])
                        s_ac.inc(ist)
                    R[("qk_evac", tc)] = s_ac.v

                if "p2" in phases:
                    for h in range(HPC):
                        _emit_attn_tb(block, R, h, tc, q_sb, k_sb, v_sb, p_sb,
                                      ao_sb, dinv_sb, bc_sb, causal_b,
                                      ones_col_b, ones_row_f,
                                      wides, pAB, pbC, pbD, cc_in,
                                      s_pe, s_ac, s_dv, s_ao, ln_row)
            if "p2" in phases:
                R["cc_in_done"] = (s_ao[0].v, s_ao[1].v)

        @block.gpsimd
        def _(g):
            s_gld.inc(g.dma_start(out=wo_bf.ap(), in_=wo), 16)
            for q4 in range(4):
                s_gld.inc(g.dma_start(out=wd_bf[q4 * 2048:(q4 + 1) * 2048, :],
                                      in_=wdown[q4 * 2048:(q4 + 1) * 2048, :]), 16)
        R["wbf_conv"] = s_gld.v

        if "p3" in phases:
            @block.gpsimd
            def _(g):
                cd = R.get("cc_in_done", (0, 0))
                if cd[0]:
                    g.wait_ge(s_ao[0].sem, cd[0])
                if cd[1]:
                    g.wait_ge(s_ao[1].sem, cd[1])
                ist = g.collective_compute(
                    "AllToAll", ALU.bypass,
                    replica_groups=[list(range(NCORE))],
                    ins=[cc_in.ap().opt()],
                    outs=[cc_out.ap().opt()])
                s_cc.inc_default(ist)

        if "p4" in phases:
            for half in range(2):
                _emit_p4_half(block, R, half, aoh[half], wo_buf, z_sb[half], x1bf,
                              bo_sb, zbf_t, sq_t, mean_sb, msq_sb, tmp_row,
                              ln_row, rstd_row, meanbc, rstdbc, epst,
                              ones_col_b, ones_row_f, wide0, pAB, pbC, pbD,
                              cc_out, xres, wo_bf, x1_dram, R["wbf_conv"], s_gld,
                              s_cc, s_sld, s_ao, s_x1, s_wo, s_pe, s_ac, s_dv)

        if "p5" in phases:
            _emit_p5a(block, R, x1bf, wg_buf, wu_buf, sg_t, h_t, A_t,
                      pM01, pm01, pM23, pm23, thr_t, mask_t, hsp_t,
                      wide0, wide1, wgate, wup, h_dram,
                      s_wgu, s_hs, s_x1, s_pe, s_ac, s_dv)
            _emit_p5b(block, R, hfm_q, wd_buf, z2_sb,
                      zbf_t, sq_t, mean_sb, msq_sb, tmp_row, ln_row,
                      rstd_row, meanbc, rstdbc, epst, ones_col_b,
                      ones_row_f, wide0, wide1, pAB, pbC, pbD,
                      wd_bf, h_dram, x1_dram, R["wbf_conv"], s_gld,
                      s_wd, s_hT, s_sld, s_hs, s_x1, s_pe, s_ac, s_dv)
            if "p6" in phases:
                for half in range(2):
                    _emit_p6_half(block, R, half, z2_sb, panel, identf, wide1,
                                  y_out, s_pe, s_dv, s_ot)

        @block.sync
        def _(s):
            for ctr in (s_ao[0], s_ao[1], s_x1[0], s_x1[1], s_hs[0], s_hs[1],
                        s_ot[0], s_ot[1]):
                if ctr.v:
                    s.wait_ge(ctr.sem, ctr.v)

    block.replay(nc)
    return nc


def _emit_attn_tb(block, R, h, tb, q_sb, k_sb, v_sb, p_sb, ao_sb, dinv_sb,
                  bc_sb, causal_b, ones_col_b, ones_row_f, wides, pAB, pbC,
                  pbD, cc_in, s_pe, s_ac, s_dv, s_ao, ln_row):
    nsb = 4 * tb + 4
    npair = nsb // 2
    po = pAB[h]
    nh, ntb = (1, tb) if h == 0 else (0, tb + 1)

    def mmS(t, sb):
        spar2 = (sb // 2) % 2
        half = sb % 2
        col0 = max(0, sb * PART - tb * 512)
        ist = t.matmul(wides[spar2][:, half * 512 + col0: (half + 1) * 512],
                       k_sb[:, h * S + sb * PART: h * S + (sb + 1) * PART],
                       q_sb[:, h * S + tb * 512 + col0: h * S + (tb + 1) * 512],
                       start=True, stop=True)
        R[("mmS", h, tb, sb)] = s_pe.inc(ist)

    def avd(t, sb):
        spar2 = (sb // 2) % 2
        half = sb % 2
        col0 = max(0, sb * PART - tb * 512)
        base = spar2 * 1024 + half * 512
        if sb >= 4 * tb:
            t.wait_ge(s_dv.sem, R[("mask", h, tb, sb)])
        t.matmul(po[:, col0:512],
                 v_sb[:, sb * 256 + h * 128: sb * 256 + (h + 1) * 128],
                 p_sb[:, base + col0: base + 512],
                 start=(sb == 0), stop=(sb == nsb - 1))
        ist = t.matmul(pbC[0:1, col0:512], ones_col_b[:, :],
                       p_sb[:, base + col0: base + 512],
                       start=(sb == 0), stop=(sb == nsb - 1))
        R[("avd", h, tb, sb)] = s_pe.inc(ist)

    @block.tensor
    def _(t):
        t.wait_ge(s_ac.sem, R[("qk_evac", tb)])
        t.wait_ge(s_dv.sem, R[("v_evac_tb", tb, 3)])
        kd = R.get(("dinv_prev", h, tb))
        if kd is not None:
            t.wait_ge(s_ac.sem, kd)
        mmS(t, 0)
        mmS(t, 1)

    for k in range(npair):
        kpar = k % 2

        @block.scalar
        def _(sc, k=k, kpar=kpar):
            if k >= 2:
                sc.wait_ge(s_pe.sem, R[("avd", h, tb, 2 * (k - 2) + 1)])
            else:
                kp = R.get(("p_free", h, tb, kpar))
                if kp is not None:
                    sc.wait_ge(s_pe.sem, kp)
            sc.wait_ge(s_pe.sem, R[("mmS", h, tb, 2 * k + 1)])
            col0 = max(0, 2 * k * PART - tb * 512)
            ist = sc.activation(p_sb[:, kpar * 1024 + col0: (kpar + 1) * 1024],
                                wides[kpar][:, col0:1024], AF.Exp, scale=QSCALE)
            R[("exp", h, tb, k)] = s_ac.inc(ist)

        dsbs = [sb for sb in (2 * k, 2 * k + 1) if sb >= 4 * tb]
        if dsbs:
            @block.vector
            def _(v, k=k, kpar=kpar, dsbs=dsbs):
                v.wait_ge(s_ac.sem, R[("exp", h, tb, k)])
                for sb in dsbs:
                    half = sb % 2
                    col0 = sb * PART - tb * 512
                    sl = p_sb[:, kpar * 1024 + half * 512 + col0:
                              kpar * 1024 + half * 512 + col0 + PART]
                    ist = v.tensor_mul(sl, sl, causal_b[:, :])
                    R[("mask", h, tb, sb)] = s_dv.inc(ist)

        @block.tensor
        def _(t, k=k, kpar=kpar, dsbs=dsbs):
            if k + 1 < npair:
                if k >= 1:
                    t.wait_ge(s_ac.sem, R[("exp", h, tb, k - 1)])
                else:
                    kw = R.get(("exp_last_par", h, tb, 1))
                    if kw is not None:
                        t.wait_ge(s_ac.sem, kw)
                mmS(t, 2 * k + 2)
                mmS(t, 2 * k + 3)
            if not dsbs:
                t.wait_ge(s_ac.sem, R[("exp", h, tb, k)])
            avd(t, 2 * k)
            avd(t, 2 * k + 1)

    R[("exp_last", h, tb)] = R[("exp", h, tb, npair - 1)]
    R[("exp_last_par", nh, ntb, (npair - 1) % 2)] = R[("exp", h, tb, npair - 1)]
    if npair >= 2:
        R[("exp_last_par", nh, ntb, (npair - 2) % 2)] = R[("exp", h, tb, npair - 2)]
    else:
        prev = R.get(("exp_last_par", h, tb, 1))
        if prev is not None:
            R[("exp_last_par", nh, ntb, 1)] = prev
    R[("p_free", nh, ntb, (npair - 1) % 2)] = R[("avd", h, tb, nsb - 1)]
    if npair >= 2:
        R[("p_free", nh, ntb, (npair - 2) % 2)] = R[("avd", h, tb, nsb - 3)]

    @block.scalar
    def _(sc):
        sc.wait_ge(s_pe.sem, R[("avd", h, tb, nsb - 1)])
        s_ac.inc(sc.activation(ln_row[0:1, :], pbC[0:1, :], AF.Ln))
        sc.wait_ge(s_ac.sem, s_ac.v)
        ist = sc.activation(dinv_sb[0:1, :], ln_row[0:1, :], AF.Exp, scale=-1.0)
        R[("dinv", h, tb)] = s_ac.inc(ist)
        R[("dinv_prev", nh, ntb)] = s_ac.v

    @block.tensor
    def _(t):
        t.wait_ge(s_ac.sem, R[("dinv", h, tb)])
        kb = R.get(("bc_free", h, tb))
        if kb is not None:
            t.wait_ge(s_dv.sem, kb)
        ist = t.matmul(pbD[:, :], ones_row_f[0:1, :], dinv_sb[0:1, :],
                       start=True, stop=True)
        R[("bc", h, tb)] = s_pe.inc(ist)

    @block.vector
    def _(v):
        v.wait_ge(s_pe.sem, R[("bc", h, tb)])
        s_dv.inc(v.tensor_copy(bc_sb[:, :], pbD[:, :]))
        R[("bc_free", nh, ntb)] = s_dv.v
        ka = R.get(("ao_store_prev", h, tb))
        if ka is not None:
            v.wait_ge(s_ao[h].sem, ka)
        v.wait_ge(s_dv.sem, s_dv.v)
        ist = v.tensor_mul(ao_sb[:, h * 512:(h + 1) * 512], po[:, :],
                           bc_sb[:, :])
        R[("ao", h, tb)] = s_dv.inc(ist)

    @block.sync
    def _(s):
        s.wait_ge(s_dv.sem, R[("ao", h, tb)])
        core_j, piece = tb // 2, tb % 2
        ist = s.dma_start(
            out=cc_in[256 * core_j + 128 * h: 256 * core_j + 128 * (h + 1),
                      piece * 512:(piece + 1) * 512],
            in_=ao_sb[:, h * 512:(h + 1) * 512])
        s_ao[h].inc(ist, 16)
        R[("ao_store_prev", h, tb + 1)] = s_ao[h].v


def _emit_p4_half(block, R, half, aoh, wo_buf, z_sb, x1bf, bo_sb,
                  zbf_t, sq_t, mean_sb, msq_sb, tmp_row, ln_row, rstd_row,
                  meanbc, rstdbc, epst, ones_col_b, ones_row_f, wide0, pAB,
                  pbC, pbD, cc_out, xres, wo_bf, x1_dram, wbf_v, s_gld,
                  s_cc, s_sld, s_ao, s_x1, s_wo, s_pe, s_ac, s_dv):
    tag = ("ln1", half)

    @block.sync
    def _(s):
        if half == 0:
            s.wait_ge(s_cc.sem, 1)
        ist = s.dma_start(out=aoh.rearrange("p (c t) -> p c t", t=512),
                          in_=cc_out[:, half * 512:(half + 1) * 512]
                              .rearrange("(c p) t -> p c t", p=PART))
        s_sld.inc(ist, 16)
        ist = s.dma_start(out=z_sb.rearrange("p (c t) -> p c t", t=512),
                          in_=xres[:, half * 512:(half + 1) * 512]
                              .rearrange("(c p) t -> p c t", p=PART))
        s_sld.inc(ist, 16)
        R[("p4_loads", half)] = s_sld.v

    for jc in range(KCH):
        par = jc % 2
        idx = half * KCH + jc

        @block.sync
        def _(s, jc=jc, par=par, idx=idx):
            if idx >= 2:
                s.wait_ge(s_pe.sem, R[("wo_used", idx - 2)])
            elif half == 0 and jc == 0:
                cd = R.get("cc_in_done", (0, 0))
                if cd[0]:
                    s.wait_ge(s_ao[0].sem, cd[0])
                if cd[1]:
                    s.wait_ge(s_ao[1].sem, cd[1])
                s.wait_ge(s_gld.sem, wbf_v)
            ist = s.dma_start(out=wo_buf[par].rearrange("p (c j) -> p c j", j=PART),
                              in_=wo_bf[:, jc * PART:(jc + 1) * PART]
                                  .rearrange("(c p) j -> p c j", p=PART))
            s_wo[par].inc(ist, 16)
            R[("wo_ld", idx)] = s_wo[par].v

        @block.tensor
        def _(t, jc=jc, par=par, idx=idx):
            if jc == 0:
                t.wait_ge(s_sld.sem, R[("p4_loads", half)])
            t.wait_ge(s_wo[par].sem, R[("wo_ld", idx)])
            if jc >= 2:
                t.wait_ge(s_dv.sem, R[("z_evac", half, jc - 2)])
            for hc in range(KCH):
                ist = t.matmul(pAB[par][:, :],
                               wo_buf[par][:, hc * PART:(hc + 1) * PART],
                               aoh[:, hc * 512:(hc + 1) * 512],
                               start=(hc == 0), stop=(hc == KCH - 1))
            R[("wo_mm", half, jc)] = s_pe.inc(ist)
            R[("wo_used", idx)] = s_pe.v
            if jc >= 2:
                _stats_mm(t, R, tag, jc - 2, zbf_t, sq_t, ones_col_b, pbC, pbD,
                          s_ac, s_pe)

        @block.vector
        def _(v, jc=jc, par=par):
            v.wait_ge(s_pe.sem, R[("wo_mm", half, jc)])
            if jc == 0:
                v.wait_ge(s_sld.sem, R[("p4_loads", half)])
            ist = v.scalar_tensor_tensor(z_sb[:, jc * 512:(jc + 1) * 512],
                                         pAB[par][:, :], bo_sb[:, jc:jc + 1],
                                         z_sb[:, jc * 512:(jc + 1) * 512],
                                         ALU.add, ALU.add)
            R[("z_evac", half, jc)] = s_dv.inc(ist)

        @block.scalar
        def _(sc, jc=jc):
            tpar = jc % 2
            sc.wait_ge(s_dv.sem, R[("z_evac", half, jc)])
            if jc >= 2:
                sc.wait_ge(s_pe.sem, R[("stat_mm", tag, jc - 2)])
            ist = sc.activation(zbf_t[:, tpar * 512:(tpar + 1) * 512],
                                z_sb[:, jc * 512:(jc + 1) * 512], AF.Identity)
            s_ac.inc(ist)
            ist = sc.activation(sq_t[:, tpar * 512:(tpar + 1) * 512],
                                z_sb[:, jc * 512:(jc + 1) * 512], AF.Square)
            R[("zbfsq", tag, jc)] = s_ac.inc(ist)

    @block.tensor
    def _(t):
        _stats_mm(t, R, tag, KCH - 2, zbf_t, sq_t, ones_col_b, pbC, pbD, s_ac, s_pe)
        _stats_mm(t, R, tag, KCH - 1, zbf_t, sq_t, ones_col_b, pbC, pbD, s_ac, s_pe)
        R[("wo_mm_done", half)] = s_pe.v

    _ln_finish(block, R, tag, z_sb, mean_sb, msq_sb, tmp_row, ln_row, rstd_row,
               meanbc, rstdbc, epst, ones_row_f, wide0, pbC, pbD, s_pe, s_ac, s_dv,
               bc_mean=(wide0[:, 0:512] if half == 0 else pAB[0][:, :]),
               bc_rstd=(wide0[:, 512:1024] if half == 0 else pAB[1][:, :]))

    @block.scalar
    def _(sc):
        for jc in range(KCH):
            sc.wait_ge(s_dv.sem, R[("xnorm", tag, jc)])
            ist = sc.activation(
                x1bf[:, jc * TC + half * 512: jc * TC + (half + 1) * 512],
                z_sb[:, jc * 512:(jc + 1) * 512], AF.Identity)
            s_ac.inc(ist)
        R[("x1bf_done", half)] = s_ac.v

    @block.sync
    def _(s):
        s.wait_ge(s_dv.sem, R[("ln_done", tag)])
        ist = s.dma_start(out=x1_dram[:, half * 512:(half + 1) * 512]
                          .rearrange("(c p) t -> p c t", p=PART),
                          in_=z_sb.rearrange("p (c t) -> p c t", t=512))
        s_x1[half].inc(ist, 16)
        R[("x1_store", half)] = s_x1[half].v


def _stats_mm(t, R, tag, jc, zbf_t, sq_t, ones_col_b, pbC, pbD, s_ac, s_pe):
    tpar = jc % 2
    t.wait_ge(s_ac.sem, R[("zbfsq", tag, jc)])
    t.matmul(pbC[0:1, :], ones_col_b[:, :], zbf_t[:, tpar * 512:(tpar + 1) * 512],
             start=(jc == 0), stop=(jc == KCH - 1))
    ist = t.matmul(pbD[0:1, :], ones_col_b[:, :], sq_t[:, tpar * 512:(tpar + 1) * 512],
                   start=(jc == 0), stop=(jc == KCH - 1))
    R[("stat_mm", tag, jc)] = s_pe.inc(ist)


def _ln_finish(block, R, tag, zbuf, mean_sb, msq_sb, tmp_row, ln_row, rstd_row,
               meanbc, rstdbc, epst, ones_row_f, widebc, pbC, pbD,
               s_pe, s_ac, s_dv, slice_fn=None, stat_mean=None, stat_sq=None,
               bc_mean=None, bc_rstd=None):
    if slice_fn is None:
        slice_fn = lambda jc: zbuf[:, jc * 512:(jc + 1) * 512]
    if stat_mean is None:
        stat_mean = pbC[0:1, :]
    if stat_sq is None:
        stat_sq = pbD[0:1, :]
    if bc_mean is None:
        bc_mean = widebc[:, 0:512]
    if bc_rstd is None:
        bc_rstd = widebc[:, 512:1024]

    @block.vector
    def _(v):
        v.wait_ge(s_pe.sem, R[("stat_mm", tag, KCH - 1)])
        s_dv.inc(v.tensor_scalar(mean_sb[0:1, :], stat_mean, 1.0 / D, None, ALU.mult))
        s_dv.inc(v.tensor_scalar(msq_sb[0:1, :], stat_sq, 1.0 / D, None, ALU.mult))
        R[("rows_done", tag)] = s_dv.v
        v.wait_ge(s_dv.sem, s_dv.v)
        s_dv.inc(v.tensor_mul(tmp_row[0:1, :], mean_sb[0:1, :], mean_sb[0:1, :]))
        v.wait_ge(s_dv.sem, s_dv.v)
        ist = v.tensor_sub(tmp_row[0:1, :], msq_sb[0:1, :], tmp_row[0:1, :])
        R[("var", tag)] = s_dv.inc(ist)

    @block.scalar
    def _(sc):
        sc.wait_ge(s_dv.sem, R[("var", tag)])
        s_ac.inc(sc.activation(ln_row[0:1, :], tmp_row[0:1, :], AF.Ln,
                               bias=epst[0:1, 0:1]))
        sc.wait_ge(s_ac.sem, s_ac.v)
        ist = sc.activation(rstd_row[0:1, :], ln_row[0:1, :], AF.Exp, scale=-0.5)
        R[("rstd", tag)] = s_ac.inc(ist)

    @block.tensor
    def _(t):
        t.wait_ge(s_ac.sem, R[("rstd", tag)])
        kx = R.get(("bc_extra_dv", tag))
        if kx is not None:
            t.wait_ge(s_dv.sem, kx)
        t.matmul(bc_mean, ones_row_f[0:1, :], mean_sb[0:1, :],
                 start=True, stop=True)
        ist = t.matmul(bc_rstd, ones_row_f[0:1, :], rstd_row[0:1, :],
                       start=True, stop=True)
        R[("bcast", tag)] = s_pe.inc(ist)

    @block.vector
    def _(v):
        v.wait_ge(s_pe.sem, R[("bcast", tag)])
        s_dv.inc(v.tensor_copy(meanbc[:, :], bc_mean))
        s_dv.inc(v.tensor_copy(rstdbc[:, :], bc_rstd))
        v.wait_ge(s_dv.sem, s_dv.v)
        for jc in range(KCH):
            sl = slice_fn(jc)
            s_dv.inc(v.tensor_sub(sl, sl, meanbc[:, :]))
            v.wait_ge(s_dv.sem, s_dv.v)
            ist = v.tensor_mul(sl, sl, rstdbc[:, :])
            R[("xnorm", tag, jc)] = s_dv.inc(ist)
        R[("ln_done", tag)] = s_dv.v


def _emit_p5a(block, R, x1bf, wg_buf, wu_buf, sg_t, h_t, A_t,
              pM01, pm01, pM23, pm23, thr_t, mask_t, hsp_t,
              wide0, wide1, wgate, wup, h_dram,
              s_wgu, s_hs, s_x1, s_pe, s_ac, s_dv):
    NJC = INTER // 512
    NTB8 = TC // PART

    for jc in range(NJC):
        par = jc % 2

        @block.gpsimd
        def _(g, jc=jc, par=par):
            if jc < 2:
                g.wait_ge(s_pe.sem, R[("wo_mm_done", 1)])
                g.wait_ge(s_x1[0].sem, R[("x1_store", 0)])
                g.wait_ge(s_ac.sem, R[("x1bf_done", 0)])
            else:
                g.wait_ge(s_pe.sem, R[("gu_used", jc - 2)])
            ist = g.dma_start(out=wg_buf[par].rearrange("p (c j) -> p c j", j=512),
                              in_=wgate[:, jc * 512:(jc + 1) * 512]
                                  .rearrange("(c p) j -> p c j", p=PART))
            s_wgu[par].inc(ist, 16)
            ist = g.dma_start(out=wu_buf[par].rearrange("p (c j) -> p c j", j=512),
                              in_=wup[:, jc * 512:(jc + 1) * 512]
                                  .rearrange("(c p) j -> p c j", p=PART))
            s_wgu[par].inc(ist, 16)
            R[("gu_ld", jc)] = s_wgu[par].v

        for tb8 in range(NTB8):
            it = jc * NTB8 + tb8
            ppar = it % 2

            @block.tensor
            def _(t, jc=jc, par=par, tb8=tb8, it=it, ppar=ppar):
                if tb8 == 0:
                    t.wait_ge(s_wgu[par].sem, R[("gu_ld", jc)])
                if tb8 == 0 and jc == 0:
                    t.wait_ge(s_ac.sem, R[("x1bf_done", 0)])
                if tb8 == 4 and jc == 0:
                    t.wait_ge(s_ac.sem, R[("x1bf_done", 1)])
                if it >= 2:
                    t.wait_ge(s_ac.sem, R[("silu", it - 2)])
                    t.wait_ge(s_dv.sem, R[("hmul", it - 2)])
                for kc in range(KCH):
                    lhs = x1bf[:, kc * TC + tb8 * PART: kc * TC + (tb8 + 1) * PART]
                    t.matmul(wide0[:, ppar * 512:(ppar + 1) * 512], lhs,
                             wg_buf[par][:, kc * 512:(kc + 1) * 512],
                             start=(kc == 0), stop=(kc == KCH - 1))
                    ist = t.matmul(wide1[:, ppar * 512:(ppar + 1) * 512], lhs,
                                   wu_buf[par][:, kc * 512:(kc + 1) * 512],
                                   start=(kc == 0), stop=(kc == KCH - 1))
                R[("gu_mm", it)] = s_pe.inc(ist)
                if tb8 == NTB8 - 1:
                    R[("gu_used", jc)] = s_pe.v

            @block.scalar
            def _(sc, it=it, ppar=ppar):
                sc.wait_ge(s_pe.sem, R[("gu_mm", it)])
                if it >= 2:
                    sc.wait_ge(s_dv.sem, R[("hmul", it - 2)])
                ist = sc.activation(sg_t[:, ppar * 512:(ppar + 1) * 512],
                                    wide0[:, ppar * 512:(ppar + 1) * 512], AF.Silu)
                R[("silu", it)] = s_ac.inc(ist)

            @block.vector
            def _(v, jc=jc, tb8=tb8, it=it, ppar=ppar):
                v.wait_ge(s_ac.sem, R[("silu", it)])
                v.wait_ge(s_dv.sem, s_dv.v)
                ist = v.tensor_tensor(h_t[:, :], sg_t[:, ppar * 512:(ppar + 1) * 512],
                                      wide1[:, ppar * 512:(ppar + 1) * 512], ALU.mult)
                R[("hmul", it)] = s_dv.inc(ist)
                v.wait_ge(s_dv.sem, s_dv.v)
                s_dv.inc(v.scalar_tensor_tensor(A_t[:, :], h_t[:, :], -1.0,
                                                h_t[:, :], ALU.mult, ALU.max))
                v.wait_ge(s_dv.sem, s_dv.v)
                A4 = A_t.ap().rearrange("p (g i) -> p g i", i=4)
                s_dv.inc(v.tensor_tensor(pM01[:, :], A4[:, :, 0], A4[:, :, 1], ALU.max))
                s_dv.inc(v.tensor_tensor(pm01[:, :], A4[:, :, 0], A4[:, :, 1], ALU.min))
                s_dv.inc(v.tensor_tensor(pM23[:, :], A4[:, :, 2], A4[:, :, 3], ALU.max))
                s_dv.inc(v.tensor_tensor(pm23[:, :], A4[:, :, 2], A4[:, :, 3], ALU.min))
                v.wait_ge(s_dv.sem, s_dv.v)
                s_dv.inc(v.tensor_tensor(pM01[:, :], pM01[:, :], pM23[:, :], ALU.min))
                s_dv.inc(v.tensor_tensor(pm01[:, :], pm01[:, :], pm23[:, :], ALU.max))
                v.wait_ge(s_dv.sem, s_dv.v)
                s_dv.inc(v.tensor_tensor(thr_t[:, :], pM01[:, :], pm01[:, :], ALU.max))
                v.wait_ge(s_dv.sem, s_dv.v)
                M4 = mask_t.ap().rearrange("p (g i) -> p g i", i=4)
                for i in range(4):
                    s_dv.inc(v.tensor_tensor(M4[:, :, i], A4[:, :, i], thr_t[:, :],
                                             ALU.is_ge))
                v.wait_ge(s_dv.sem, s_dv.v)
                ks = R.get(("hsp_store", it - 2))
                if ks is not None:
                    v.wait_ge(s_hs[it % 2].sem, ks)
                ist = v.tensor_tensor(hsp_t[:, ppar * 512:(ppar + 1) * 512],
                                      h_t[:, :], mask_t[:, :], ALU.mult)
                R[("hsp", it)] = s_dv.inc(ist)

            @block.sync
            def _(s, jc=jc, tb8=tb8, it=it, ppar=ppar):
                s.wait_ge(s_dv.sem, R[("hsp", it)])
                ist = s.dma_start(out=h_dram[tb8 * PART:(tb8 + 1) * PART,
                                             jc * 512:(jc + 1) * 512],
                                  in_=hsp_t[:, ppar * 512:(ppar + 1) * 512])
                s_hs[ppar].inc(ist, 16)
                R[("hsp_store", it)] = s_hs[ppar].v

    R["h_all_stored"] = (s_hs[0].v, s_hs[1].v)
    R["gu_all_used"] = s_pe.v


def _emit_p5b(block, R, hfm_q, wd_buf, z2_sb,
              zbf_t, sq_t, mean_sb, msq_sb, tmp_row, ln_row, rstd_row,
              meanbc, rstdbc, epst, ones_col_b, ones_row_f,
              wide0, wide1, pAB, pbC, pbD, wd_bf, h_dram, x1_dram, wbf_v, s_gld,
              s_wd, s_hT, s_sld, s_hs, s_x1, s_pe, s_ac, s_dv):
    _DOWN_PSUM = [pAB[0], pAB[1], wide0[:, 0:512], wide0[:, 512:1024]]

    @block.scalar
    def _(sc):
        sc.wait_ge(s_pe.sem, R["gu_all_used"])
        sc.wait_ge(s_x1[0].sem, R[("x1_store", 0)])
        sc.wait_ge(s_x1[1].sem, R[("x1_store", 1)])
        ist = sc.dma_start(out=z2_sb.rearrange("p (c t) -> p c t", t=TC),
                          in_=x1_dram.rearrange("(c p) t -> p c t", p=PART))
        s_sld.inc(ist, 16)
        R["z2_init"] = s_sld.v

    for ipart in range(4):
        hpar = ipart % 2

        @block.sync
        def _(s, ipart=ipart, hpar=hpar):
            if ipart == 0:
                hs0, hs1 = R["h_all_stored"]
                if hs0:
                    s.wait_ge(s_hs[0].sem, hs0)
                if hs1:
                    s.wait_ge(s_hs[1].sem, hs1)
            if ipart >= 2:
                s.wait_ge(s_pe.sem, R[("down_pe", ipart - 2)])
            for i16 in range(16):
                ic = ipart * 16 + i16
                ist = s.dma_start_transpose(
                    hfm_q[hpar][:, i16 * 1024:(i16 + 1) * 1024],
                    h_dram[:, ic * PART:(ic + 1) * PART])
                s_hT[hpar].inc(ist, 16)
            R[("hT", ipart)] = s_hT[hpar].v

        for j2 in range(KCH):
            wpar = j2 % 2
            idx = ipart * KCH + j2

            @block.gpsimd
            def _(g, ipart=ipart, j2=j2, wpar=wpar, idx=idx):
                if idx >= 2:
                    g.wait_ge(s_pe.sem, R[("wd_used", idx - 2)])
                elif idx == 0:
                    g.wait_ge(s_pe.sem, R["gu_all_used"])
                    g.wait_ge(s_gld.sem, wbf_v)
                ist = g.dma_start(
                    out=wd_buf[wpar].rearrange("p (c j) -> p c j", j=PART),
                    in_=wd_bf[ipart * 2048:(ipart + 1) * 2048,
                              j2 * PART:(j2 + 1) * PART]
                        .rearrange("(c p) j -> p c j", p=PART))
                s_wd[wpar].inc(ist, 16)
                R[("wd_ld", idx)] = s_wd[wpar].v

            @block.tensor
            def _(t, ipart=ipart, j2=j2, wpar=wpar, idx=idx, hpar=hpar):
                if j2 == 0:
                    t.wait_ge(s_hT[hpar].sem, R[("hT", ipart)])
                t.wait_ge(s_wd[wpar].sem, R[("wd_ld", idx)])
                for th in range(2):
                    pos = j2 * 2 + th
                    dtile = _DOWN_PSUM[pos % 4]
                    kz = R.get(("z2_add", ipart, pos - 4))
                    if kz is None and ipart >= 1:
                        kz = R.get(("z2_add", ipart - 1, 28 + pos))
                    if kz is not None:
                        t.wait_ge(s_dv.sem, kz)
                    for c in range(16):
                        ist = t.matmul(dtile[:, :],
                                       wd_buf[wpar][:, c * PART:(c + 1) * PART],
                                       hfm_q[hpar][:, c * 1024 + th * 512:
                                                   c * 1024 + (th + 1) * 512],
                                       start=(c == 0), stop=(c == 15))
                    R[("down_mm", ipart, pos)] = s_pe.inc(ist)
                R[("wd_used", idx)] = s_pe.v

            @block.vector
            def _(v, ipart=ipart, j2=j2):
                if ipart == 0 and j2 == 0:
                    v.wait_ge(s_sld.sem, R["z2_init"])
                for th in range(2):
                    pos = j2 * 2 + th
                    dtile = _DOWN_PSUM[pos % 4]
                    v.wait_ge(s_pe.sem, R[("down_mm", ipart, pos)])
                    sl = z2_sb[:, j2 * TC + th * 512: j2 * TC + (th + 1) * 512]
                    ist = v.tensor_add(sl, sl, dtile[:, :])
                    R[("z2_add", ipart, pos)] = s_dv.inc(ist)

            if ipart == 3:
                @block.scalar
                def _(sc, j2=j2):
                    for th in range(2):
                        tag = ("ln2", th)
                        sc.wait_ge(s_dv.sem, R[("z2_add", 3, j2 * 2 + th)])
                        if j2 >= 1:
                            sc.wait_ge(s_pe.sem, R[("stat_mm", tag, j2 - 1)])
                        sl = z2_sb[:, j2 * TC + th * 512: j2 * TC + (th + 1) * 512]
                        ist = sc.activation(zbf_t[:, th * 512:(th + 1) * 512],
                                            sl, AF.Identity)
                        s_ac.inc(ist)
                        ist = sc.activation(sq_t[:, th * 512:(th + 1) * 512],
                                            sl, AF.Square)
                        R[("zbfsq", tag, j2)] = s_ac.inc(ist)

                @block.tensor
                def _(t, j2=j2):
                    for th in range(2):
                        tag = ("ln2", th)
                        mrow = pbC[0:1, :] if th == 0 else wide1[0:1, 0:512]
                        qrow = pbD[0:1, :] if th == 0 else wide1[0:1, 512:1024]
                        t.wait_ge(s_ac.sem, R[("zbfsq", tag, j2)])
                        t.matmul(mrow, ones_col_b[:, :],
                                 zbf_t[:, th * 512:(th + 1) * 512],
                                 start=(j2 == 0), stop=(j2 == KCH - 1))
                        ist = t.matmul(qrow, ones_col_b[:, :],
                                       sq_t[:, th * 512:(th + 1) * 512],
                                       start=(j2 == 0), stop=(j2 == KCH - 1))
                        R[("stat_mm", tag, j2)] = s_pe.inc(ist)

        @block.tensor
        def _(t, ipart=ipart):
            R[("down_pe", ipart)] = s_pe.v

    R[("bc_extra_dv", ("ln2", 0))] = R[("z2_add", 3, 31)]
    for th in range(2):
        _ln_finish(block, R, ("ln2", th), None,
                   mean_sb, msq_sb, tmp_row, ln_row, rstd_row,
                   meanbc, rstdbc, epst, ones_row_f, wide0, pbC, pbD,
                   s_pe, s_ac, s_dv,
                   slice_fn=lambda jc, th=th: z2_sb[:, jc * TC + th * 512:
                                                    jc * TC + (th + 1) * 512],
                   stat_mean=(pbC[0:1, :] if th == 0 else wide1[0:1, 0:512]),
                   stat_sq=(pbD[0:1, :] if th == 0 else wide1[0:1, 512:1024]),
                   bc_mean=(wide0[:, 0:512] if th == 0 else pbC[:, :]),
                   bc_rstd=(wide0[:, 512:1024] if th == 0 else pbD[:, :]))
        R[("ln2_done", th)] = R[("ln_done", ("ln2", th))]


def _emit_p6_half(block, R, half, z2_sb, panel, identf, wide1, y_out,
                  s_pe, s_dv, s_ot):
    for t128 in range(4):
        pidx = half * 4 + t128
        pi = pidx % 2

        for jc in range(KCH):
            ti = jc % 2

            @block.tensor
            def _(t, t128=t128, jc=jc, ti=ti):
                t.wait_ge(s_dv.sem, R[("xnorm", ("ln2", half), jc)])
                if t128 == 0 and jc == 0 and half == 0:
                    t.wait_ge(s_dv.sem, R[("rows_done", ("ln2", 1))])
                idx = t128 * KCH + jc
                ko = R.get(("outt_copied", half, idx - 2))
                if ko is None and half == 1:
                    ko = R.get(("outt_copied", 0, 4 * KCH + idx - 2))
                if ko is not None:
                    t.wait_ge(s_dv.sem, ko)
                ist = t.transpose(
                    wide1[:, ti * 512: ti * 512 + 128],
                    z2_sb[:, jc * TC + half * 512 + t128 * PART:
                          jc * TC + half * 512 + (t128 + 1) * PART],
                    identf[:, :])
                R[("tr", half, t128, jc)] = s_pe.inc(ist)

            @block.vector
            def _(v, t128=t128, jc=jc, ti=ti, pi=pi, pidx=pidx):
                v.wait_ge(s_pe.sem, R[("tr", half, t128, jc)])
                if jc == 0:
                    ks = R.get(("panel_stored", pidx - 2))
                    if ks is not None:
                        v.wait_ge(s_ot[pi].sem, ks)
                idx = t128 * KCH + jc
                ist = v.tensor_copy(panel[pi][:, jc * PART:(jc + 1) * PART],
                                    wide1[:, ti * 512: ti * 512 + 128])
                R[("outt_copied", half, idx)] = s_dv.inc(ist)

        @block.sync
        def _(s, t128=t128, pi=pi, pidx=pidx):
            s.wait_ge(s_dv.sem, R[("outt_copied", half, t128 * KCH + KCH - 1)])
            trow = half * 512 + t128 * PART
            ist = s.dma_start(out=y_out[trow:trow + PART, :], in_=panel[pi][:, :])
            s_ot[pi].inc(ist, 16)
            R[("panel_stored", pidx)] = s_ot[pi].v
    R[("out_done", half)] = (s_ot[0].v, s_ot[1].v)



_NC_CACHE = {}


def _get_nc():
    if "full" not in _NC_CACHE:
        _NC_CACHE["full"] = build_nc()
    return _NC_CACHE["full"]


def make_in_maps(L0_attn_x, L0_attn_Wqkv, L0_attn_bqkv, L0_attn_Wo, L0_attn_bo,
                 L0_ffn_w_gate, L0_ffn_w_up, L0_ffn_w_down):
    x = np.ascontiguousarray(np.asarray(L0_attn_x, dtype=np.float32).reshape(S, D).T)
    Wqkv = np.asarray(L0_attn_Wqkv, dtype=np.float32)
    bqkv = np.asarray(L0_attn_bqkv, dtype=np.float32)
    Wo = np.ascontiguousarray(np.asarray(L0_attn_Wo, dtype=np.float32))
    bo = np.ascontiguousarray(np.asarray(L0_attn_bo, dtype=np.float32))
    wg = np.ascontiguousarray(np.asarray(L0_ffn_w_gate, dtype=np.float32))
    wu = np.ascontiguousarray(np.asarray(L0_ffn_w_up, dtype=np.float32))
    wd = np.ascontiguousarray(np.asarray(L0_ffn_w_down, dtype=np.float32))
    in_maps = []
    for c in range(NCORE):
        cs = 256 * c
        in_maps.append({
            "xT": x,
            "wq": np.ascontiguousarray(Wqkv[:, cs:cs + 256]),
            "wk": np.ascontiguousarray(Wqkv[:, 2048 + cs:2048 + cs + 256]),
            "wv": np.ascontiguousarray(Wqkv[:, 4096 + cs:4096 + cs + 256]),
            "bq": np.ascontiguousarray(bqkv[cs:cs + 256]),
            "bk": np.ascontiguousarray(bqkv[2048 + cs:2048 + cs + 256]),
            "bv": np.ascontiguousarray(bqkv[4096 + cs:4096 + cs + 256]),
            "wo": Wo,
            "bo": bo,
            "xres": np.ascontiguousarray(x[:, TC * c:TC * (c + 1)]),
            "wgate": wg,
            "wup": wu,
            "wdown": wd,
        })
    return in_maps


def kernel(**inputs):
    in_maps = make_in_maps(**inputs)
    nc = _get_nc()
    res = run_bass_kernel_spmd(nc, in_maps, core_ids=list(range(NCORE)))
    y = np.concatenate([np.asarray(res.results[c]["y"], dtype=np.float32)
                        for c in range(NCORE)], axis=0)
    return y.reshape(1, S, D)


# version 09
# speedup vs baseline: 1.0171x; 1.0171x over previous
import os
os.environ.setdefault("TRN_ALLOW_ICI_ALL_TO_ALL", "1")

import numpy as np
import concourse.bass as bass
import concourse.mybir as mybir
from concourse.bass_utils import run_bass_kernel_spmd

F32 = mybir.dt.float32
BF16 = mybir.dt.bfloat16
AF = mybir.ActivationFunctionType
ALU = mybir.AluOpType

D = 2048
S = 8192
NH = 16
HD = 128
NCORE = 8
HPC = 2
TC = 1024
INTER = 8192
KCH = D // 128
TCHUNK = 512
NTCH = S // TCHUNK
NTB = S // 512
QSCALE = 1.0 / float(np.sqrt(HD))
EPS = 1e-5
PART = 128



class _RecInst:
    __slots__ = ("incs",)

    def __init__(self):
        self.incs = []

    def then_inc(self, sem, n=None):
        self.incs.append((sem, n))
        return self


class _RecEngine:
    def __init__(self, log):
        self._log = log

    def __getattr__(self, meth):
        log = self._log

        def call(*a, **k):
            rec = _RecInst()
            log.append((meth, a, k, rec))
            return rec
        return call


class RecBlock:

    def __init__(self):
        self.logs = {e: [] for e in
                     ("tensor", "scalar", "vector", "gpsimd", "sync")}

    def _dec(self, engine):
        def deco(fn):
            fn(_RecEngine(self.logs[engine]))
        return deco

    @property
    def tensor(self):
        return self._dec("tensor")

    @property
    def scalar(self):
        return self._dec("scalar")

    @property
    def vector(self):
        return self._dec("vector")

    @property
    def gpsimd(self):
        return self._dec("gpsimd")

    @property
    def sync(self):
        return self._dec("sync")

    def replay(self, nc):
        with nc.Block() as blk:
            for engine in ("tensor", "scalar", "vector", "gpsimd", "sync"):
                log = self.logs[engine]

                def emit(eng, log=log):
                    for meth, a, k, rec in log:
                        inst = getattr(eng, meth)(*a, **k)
                        for sem, n in rec.incs:
                            if n is None:
                                inst.then_inc(sem)
                            else:
                                inst.then_inc(sem, n)
                getattr(blk, engine)(emit)


class Ctr:
    def __init__(self, nc, name):
        self.sem = nc.alloc_semaphore(name)
        self.v = 0

    def inc(self, inst, n=1):
        inst.then_inc(self.sem, n)
        self.v += n
        return self.v

    def inc_default(self, inst):
        inst.then_inc(self.sem)
        self.v = max(self.v, 1)
        return 1


def build_nc(phases=("p1", "p2", "p3", "p4", "p5", "p6")):
    nc = bass.Bass("TRN2", target_bir_lowering=False, debug=False)

    xT = nc.dram_tensor("xT", [D, S], F32, kind="ExternalInput").ap()
    wq = nc.dram_tensor("wq", [D, HPC * HD], F32, kind="ExternalInput").ap()
    wk = nc.dram_tensor("wk", [D, HPC * HD], F32, kind="ExternalInput").ap()
    wv = nc.dram_tensor("wv", [D, HPC * HD], F32, kind="ExternalInput").ap()
    bq = nc.dram_tensor("bq", [HPC * HD], F32, kind="ExternalInput").ap()
    bk = nc.dram_tensor("bk", [HPC * HD], F32, kind="ExternalInput").ap()
    bv = nc.dram_tensor("bv", [HPC * HD], F32, kind="ExternalInput").ap()
    wo = nc.dram_tensor("wo", [D, D], F32, kind="ExternalInput").ap()
    bo = nc.dram_tensor("bo", [D], F32, kind="ExternalInput").ap()
    xres = nc.dram_tensor("xres", [D, TC], F32, kind="ExternalInput").ap()
    wgate = nc.dram_tensor("wgate", [D, INTER], F32, kind="ExternalInput").ap()
    wup = nc.dram_tensor("wup", [D, INTER], F32, kind="ExternalInput").ap()
    wdown = nc.dram_tensor("wdown", [INTER, D], F32, kind="ExternalInput").ap()
    y_out = nc.dram_tensor("y", [TC, D], F32, kind="ExternalOutput").ap()

    cc_in = [nc.dram_tensor("cc_in0", [D, 512], BF16),
             nc.dram_tensor("cc_in1", [D, 512], BF16)]
    cc_out = [nc.dram_tensor("cc_out0", [D, 512], BF16),
              nc.dram_tensor("cc_out1", [D, 512], BF16)]
    h_dram = nc.dram_tensor("h_dram", [TC, INTER], BF16)
    x1_dram = nc.dram_tensor("x1_dram", [D, TC], F32)
    wo_bf = nc.dram_tensor("wo_bf", [D, D], BF16)
    wd_bf = nc.dram_tensor("wd_bf", [INTER, D], BF16)

    AR_BYTES = 152 * 1024
    arena = nc.alloc_sbuf_tensor("arena", [PART, AR_BYTES // 2], BF16)
    ar_bf = arena.ap()
    ar_f32 = arena.ap().bitcast(F32)

    def carve_bf(off_kb, cols):
        c0 = off_kb * 1024 // 2
        return ar_bf[:, c0:c0 + cols]

    def carve_f32(off_kb, cols):
        c0 = off_kb * 1024 // 4
        return ar_f32[:, c0:c0 + cols]

    q_sb = carve_bf(0, HPC * S)
    k_sb = carve_bf(32, HPC * S)
    v_sb = carve_bf(64, 64 * 256)
    xbuf = [carve_bf(96, KCH * TCHUNK), carve_bf(112, KCH * TCHUNK)]
    wq_sb = carve_bf(128, KCH * 256)
    wk_sb = carve_bf(136, KCH * 256)
    wv_sb = carve_bf(144, KCH * 256)
    aoh = [carve_bf(0, KCH * 512), carve_bf(16, KCH * 512)]
    wo_buf = [carve_bf(48, KCH * 128), carve_bf(52, KCH * 128)]
    z_sb = [carve_f32(56, KCH * 512), carve_f32(88, KCH * 512)]
    x1bf = carve_bf(120, KCH * TC)
    wg_buf = [carve_bf(0, KCH * 512), carve_bf(16, KCH * 512)]
    wu_buf = [carve_bf(32, KCH * 512), carve_bf(48, KCH * 512)]
    hfm_q = [carve_bf(0, 16 * 1024), carve_bf(32, 16 * 1024)]
    wd_buf = [carve_bf(64, 16 * 128), carve_bf(68, 16 * 128)]
    z2_sb = carve_f32(80, KCH * TC)
    panel = [carve_f32(0, D), carve_f32(8, D)]

    ones_col_b = nc.alloc_sbuf_tensor("ones_col_b", [PART, 1], BF16)
    ones_row_b = nc.alloc_sbuf_tensor("ones_row_b", [1, PART], BF16)
    ones_row_f = nc.alloc_sbuf_tensor("ones_row_f", [1, PART], F32)
    identf = nc.alloc_sbuf_tensor("identf", [PART, PART], F32)
    causal_b = nc.alloc_sbuf_tensor("causal_b", [PART, PART], BF16)
    epst = nc.alloc_sbuf_tensor("epst", [1, 1], F32)
    bq_sb = nc.alloc_sbuf_tensor("bq_sb", [PART, HPC], F32)
    bk_sb = nc.alloc_sbuf_tensor("bk_sb", [PART, HPC], F32)
    bv_row = nc.alloc_sbuf_tensor("bv_row", [1, HPC * HD], BF16)
    bo_sb = nc.alloc_sbuf_tensor("bo_sb", [PART, KCH], F32)

    p_sb = nc.alloc_sbuf_tensor("p_sb", [PART, 2 * 1024], BF16)
    ao_sb = nc.alloc_sbuf_tensor("ao_sb", [PART, 2 * 512], BF16)
    dinv_sb = nc.alloc_sbuf_tensor("dinv_sb", [1, 512], F32)
    bc_sb = nc.alloc_sbuf_tensor("bc_sb", [PART, 512], F32)

    mean_sb = nc.alloc_sbuf_tensor("mean_sb", [1, 512], F32)
    msq_sb = nc.alloc_sbuf_tensor("msq_sb", [1, 512], F32)
    tmp_row = nc.alloc_sbuf_tensor("tmp_row", [1, 512], F32)
    ln_row = nc.alloc_sbuf_tensor("ln_row", [1, 512], F32)
    rstd_row = nc.alloc_sbuf_tensor("rstd_row", [1, 512], F32)
    meanbc = nc.alloc_sbuf_tensor("meanbc", [PART, 512], F32)
    rstdbc = nc.alloc_sbuf_tensor("rstdbc", [PART, 512], F32)
    zbf_t = nc.alloc_sbuf_tensor("zbf_t", [PART, 2 * 512], BF16)
    sq_t = nc.alloc_sbuf_tensor("sq_t", [PART, 2 * 512], BF16)

    sg_t = nc.alloc_sbuf_tensor("sg_t", [PART, 2 * 512], F32)
    h_t = nc.alloc_sbuf_tensor("h_t", [PART, 512], F32)
    A_t = nc.alloc_sbuf_tensor("A_t", [PART, 512], F32)
    pM01 = nc.alloc_sbuf_tensor("pM01", [PART, 128], F32)
    pm01 = nc.alloc_sbuf_tensor("pm01", [PART, 128], F32)
    pM23 = nc.alloc_sbuf_tensor("pM23", [PART, 128], F32)
    pm23 = nc.alloc_sbuf_tensor("pm23", [PART, 128], F32)
    thr_t = nc.alloc_sbuf_tensor("thr_t", [PART, 128], F32)
    mask_t = nc.alloc_sbuf_tensor("mask_t", [PART, 512], BF16)
    hsp_t = nc.alloc_sbuf_tensor("hsp_t", [PART, 2 * 512], BF16)
    outt = nc.alloc_sbuf_tensor("outt", [PART, 2 * 128], F32)

    wide0 = nc.alloc_psum_tensor("wide0", [PART, 1024], F32)
    wide1 = nc.alloc_psum_tensor("wide1", [PART, 1024], F32)
    pbA = nc.alloc_psum_tensor("pbA", [PART, 512], F32)
    pbB = nc.alloc_psum_tensor("pbB", [PART, 512], F32)
    pbC = nc.alloc_psum_tensor("pbC", [PART, 512], F32)
    pbD = nc.alloc_psum_tensor("pbD", [PART, 512], F32)
    wides = [wide0, wide1]
    pAB = [pbA, pbB]

    s_gld = Ctr(nc, "s_gld")
    s_xb = [Ctr(nc, "s_xb0"), Ctr(nc, "s_xb1")]
    s_wo = [Ctr(nc, "s_wo0"), Ctr(nc, "s_wo1")]
    s_wgu = [Ctr(nc, "s_wgu0"), Ctr(nc, "s_wgu1")]
    s_wd = [Ctr(nc, "s_wd0"), Ctr(nc, "s_wd1")]
    s_sld = Ctr(nc, "s_sld")
    s_hT = [Ctr(nc, "s_hT0"), Ctr(nc, "s_hT1")]
    s_ao = [[Ctr(nc, "s_ao00"), Ctr(nc, "s_ao01")],
            [Ctr(nc, "s_ao10"), Ctr(nc, "s_ao11")]]
    s_cc2 = Ctr(nc, "s_cc2")
    s_x1 = [Ctr(nc, "s_x1a"), Ctr(nc, "s_x1b")]
    s_hs = [Ctr(nc, "s_hs0"), Ctr(nc, "s_hs1")]
    s_ot = [Ctr(nc, "s_ot0"), Ctr(nc, "s_ot1")]
    s_gc = Ctr(nc, "s_gc")
    s_pe = Ctr(nc, "s_pe")
    s_ac = Ctr(nc, "s_ac")
    s_dv = Ctr(nc, "s_dv")
    s_cc = Ctr(nc, "s_cc")

    R = {}

    if True:
        block = RecBlock()
        @block.gpsimd
        def _(g):
            g.memset(ones_col_b[:, :], 1.0)
            g.memset(ones_row_b[:, :], 1.0)
            g.memset(ones_row_f[:, :], 1.0)
            g.memset(epst[:, :], EPS)
            g.memset(identf[:, :], 1.0)
            ist = g.memset(causal_b[:, :], 1.0)
            s_gc.inc(ist)
            g.wait_ge(s_gc.sem, s_gc.v)
            g.affine_select(identf[:, :], identf[:, :], [[1, PART]], ALU.is_equal,
                            0.0, base=0, channel_multiplier=-1)
            ist = g.affine_select(causal_b[:, :], causal_b[:, :], [[1, PART]], ALU.is_ge,
                                  0.0, base=0, channel_multiplier=-1)
            s_gc.inc(ist)
            s_gld.inc(g.dma_start(out=wq_sb.rearrange("p (c j) -> p c j", j=256),
                                  in_=wq.rearrange("(c p) j -> p c j", p=PART)), 16)
            s_gld.inc(g.dma_start(out=wk_sb.rearrange("p (c j) -> p c j", j=256),
                                  in_=wk.rearrange("(c p) j -> p c j", p=PART)), 16)
            s_gld.inc(g.dma_start(out=wv_sb.rearrange("p (c j) -> p c j", j=256),
                                  in_=wv.rearrange("(c p) j -> p c j", p=PART)), 16)
            for h in range(HPC):
                s_gld.inc(g.dma_start(out=bq_sb[:, h], in_=bq[h * HD:(h + 1) * HD]), 16)
                s_gld.inc(g.dma_start(out=bk_sb[:, h], in_=bk[h * HD:(h + 1) * HD]), 16)
            s_gld.inc(g.dma_start(out=bv_row[0:1, :],
                                  in_=bv.rearrange("(a j) -> a j", a=1)), 16)
            for jc in range(KCH):
                s_gld.inc(g.dma_start(out=bo_sb[:, jc],
                                      in_=bo[jc * PART:(jc + 1) * PART]), 16)
        R["consts_gc"] = s_gc.v
        R["consts_gld"] = s_gld.v

        if "p1" in phases:
            for tc in range(NTCH):
                par = tc % 2

                @block.gpsimd
                def _(g, tc=tc, par=par):
                    if tc >= 2:
                        g.wait_ge(s_pe.sem, R[("xbuf_free", tc - 2)])
                    ist = g.dma_start(
                        out=xbuf[par].rearrange("p (c t) -> p c t", t=TCHUNK),
                        in_=xT[:, tc * TCHUNK:(tc + 1) * TCHUNK]
                            .rearrange("(c p) t -> p c t", p=PART))
                    s_xb[par].inc(ist, 16)
                    R[("xbuf", tc)] = s_xb[par].v

                @block.tensor
                def _(t, tc=tc, par=par):
                    if tc == 0:
                        t.wait_ge(s_gc.sem, R["consts_gc"])
                        t.wait_ge(s_gld.sem, R["consts_gld"])
                    t.wait_ge(s_xb[par].sem, R[("xbuf", tc)])
                    if tc >= 1:
                        t.wait_ge(s_ac.sem, R[("exp_last", 1, tc - 1)])
                    for h in range(HPC):
                        for kc in range(KCH):
                            ist = t.matmul(wide0[:, h * 512:(h + 1) * 512],
                                           wq_sb[:, (kc * 2 + h) * HD:(kc * 2 + h + 1) * HD],
                                           xbuf[par][:, kc * TCHUNK:(kc + 1) * TCHUNK],
                                           start=(kc == 0), stop=(kc == KCH - 1))
                        R[("q_done", tc, h)] = s_pe.inc(ist)
                        for kc in range(KCH):
                            ist = t.matmul(wide1[:, h * 512:(h + 1) * 512],
                                           wk_sb[:, (kc * 2 + h) * HD:(kc * 2 + h + 1) * HD],
                                           xbuf[par][:, kc * TCHUNK:(kc + 1) * TCHUNK],
                                           start=(kc == 0), stop=(kc == KCH - 1))
                        R[("k_done", tc, h)] = s_pe.inc(ist)
                for tb in range(4):
                    @block.tensor
                    def _(t, tc=tc, par=par, tb=tb):
                        vp = pAB[tb % 2]
                        if tb >= 2:
                            t.wait_ge(s_dv.sem, R[("v_evac_tb", tc, tb - 2)])
                        elif tc >= 1:
                            t.wait_ge(s_dv.sem, R[("v_evac_tb", tc - 1, tb + 2)])
                            t.wait_ge(s_dv.sem, R[("ao", tb % 2, tc - 1)])
                        for kc in range(KCH):
                            t.matmul(vp[:, 0:256],
                                     xbuf[par][:, kc * TCHUNK + tb * PART:
                                               kc * TCHUNK + (tb + 1) * PART],
                                     wv_sb[:, kc * 256:(kc + 1) * 256],
                                     start=(kc == 0), stop=False)
                        ist = t.matmul(vp[:, 0:256], ones_row_b[0:1, :],
                                       bv_row[0:1, :], start=False, stop=True)
                        R[("v_done", tc, tb)] = s_pe.inc(ist)
                        if tb == 3:
                            R[("xbuf_free", tc)] = s_pe.v

                    @block.vector
                    def _(v, tc=tc, tb=tb):
                        vp = pAB[tb % 2]
                        v.wait_ge(s_pe.sem, R[("v_done", tc, tb)])
                        tbg = tc * 4 + tb
                        ist = v.tensor_copy(v_sb[:, tbg * 256:(tbg + 1) * 256],
                                            vp[:, 0:256])
                        R[("v_evac_tb", tc, tb)] = s_dv.inc(ist)

                @block.scalar
                def _(sc, tc=tc):
                    for h in range(HPC):
                        sc.wait_ge(s_pe.sem, R[("q_done", tc, h)])
                        ist = sc.activation(
                            q_sb[:, h * S + tc * TCHUNK: h * S + (tc + 1) * TCHUNK],
                            wide0[:, h * 512:(h + 1) * 512], AF.Identity,
                            bias=bq_sb[:, h:h + 1])
                        s_ac.inc(ist)
                        sc.wait_ge(s_pe.sem, R[("k_done", tc, h)])
                        ist = sc.activation(
                            k_sb[:, h * S + tc * TCHUNK: h * S + (tc + 1) * TCHUNK],
                            wide1[:, h * 512:(h + 1) * 512], AF.Identity,
                            bias=bk_sb[:, h:h + 1])
                        s_ac.inc(ist)
                    R[("qk_evac", tc)] = s_ac.v

                if "p2" in phases:
                    for h in range(HPC):
                        _emit_attn_tb(block, R, h, tc, q_sb, k_sb, v_sb, p_sb,
                                      ao_sb, dinv_sb, bc_sb, causal_b,
                                      ones_col_b, ones_row_f,
                                      wides, pAB, pbC, pbD, cc_in,
                                      s_pe, s_ac, s_dv, s_ao, ln_row)
            if "p2" in phases:
                R["cc_in_done"] = ((s_ao[0][0].v, s_ao[1][0].v),
                                   (s_ao[0][1].v, s_ao[1][1].v))

        @block.gpsimd
        def _(g):
            s_gld.inc(g.dma_start(out=wo_bf.ap(), in_=wo), 16)
            for q4 in range(4):
                s_gld.inc(g.dma_start(out=wd_bf[q4 * 2048:(q4 + 1) * 2048, :],
                                      in_=wdown[q4 * 2048:(q4 + 1) * 2048, :]), 16)
        R["wbf_conv"] = s_gld.v

        if "p3" in phases:
            @block.gpsimd
            def _(g):
                cd = R.get("cc_in_done", ((0, 0), (0, 0)))
                for piece, ctr in ((0, s_cc), (1, s_cc2)):
                    for h in range(HPC):
                        if cd[piece][h]:
                            g.wait_ge(s_ao[h][piece].sem, cd[piece][h])
                    ist = g.collective_compute(
                        "AllToAll", ALU.bypass,
                        replica_groups=[list(range(NCORE))],
                        ins=[cc_in[piece].ap().opt()],
                        outs=[cc_out[piece].ap().opt()])
                    ctr.inc_default(ist)

        if "p4" in phases:
            for half in range(2):
                _emit_p4_half(block, R, half, aoh[half], wo_buf, z_sb[half], x1bf,
                              bo_sb, zbf_t, sq_t, mean_sb, msq_sb, tmp_row,
                              ln_row, rstd_row, meanbc, rstdbc, epst,
                              ones_col_b, ones_row_f, wide0, pAB, pbC, pbD,
                              cc_out[half], xres, wo_bf, x1_dram,
                              R["wbf_conv"], s_gld,
                              (s_cc if half == 0 else s_cc2), s_sld, s_ao,
                              s_x1, s_wo, s_pe, s_ac, s_dv)

        if "p5" in phases:
            _emit_p5a(block, R, x1bf, wg_buf, wu_buf, sg_t, h_t, A_t,
                      pM01, pm01, pM23, pm23, thr_t, mask_t, hsp_t,
                      wide0, wide1, wgate, wup, h_dram,
                      s_wgu, s_hs, s_x1, s_pe, s_ac, s_dv)
            _emit_p5b(block, R, hfm_q, wd_buf, z2_sb,
                      zbf_t, sq_t, mean_sb, msq_sb, tmp_row, ln_row,
                      rstd_row, meanbc, rstdbc, epst, ones_col_b,
                      ones_row_f, wide0, wide1, pAB, pbC, pbD,
                      wd_bf, h_dram, x1_dram, R["wbf_conv"], s_gld,
                      s_wd, s_hT, s_sld, s_hs, s_x1, s_pe, s_ac, s_dv)
            if "p6" in phases:
                for half in range(2):
                    _emit_p6_half(block, R, half, z2_sb, panel, identf, wide1,
                                  y_out, s_pe, s_dv, s_ot)

        @block.sync
        def _(s):
            for ctr in (s_ao[0][0], s_ao[0][1], s_ao[1][0], s_ao[1][1],
                        s_x1[0], s_x1[1], s_hs[0], s_hs[1],
                        s_ot[0], s_ot[1]):
                if ctr.v:
                    s.wait_ge(ctr.sem, ctr.v)

    block.replay(nc)
    return nc


def _emit_attn_tb(block, R, h, tb, q_sb, k_sb, v_sb, p_sb, ao_sb, dinv_sb,
                  bc_sb, causal_b, ones_col_b, ones_row_f, wides, pAB, pbC,
                  pbD, cc_in, s_pe, s_ac, s_dv, s_ao, ln_row):
    nsb = 4 * tb + 4
    npair = nsb // 2
    po = pAB[h]
    nh, ntb = (1, tb) if h == 0 else (0, tb + 1)

    def mmS(t, sb):
        spar2 = (sb // 2) % 2
        half = sb % 2
        col0 = max(0, sb * PART - tb * 512)
        ist = t.matmul(wides[spar2][:, half * 512 + col0: (half + 1) * 512],
                       k_sb[:, h * S + sb * PART: h * S + (sb + 1) * PART],
                       q_sb[:, h * S + tb * 512 + col0: h * S + (tb + 1) * 512],
                       start=True, stop=True)
        R[("mmS", h, tb, sb)] = s_pe.inc(ist)

    def avd(t, sb):
        spar2 = (sb // 2) % 2
        half = sb % 2
        col0 = max(0, sb * PART - tb * 512)
        base = spar2 * 1024 + half * 512
        if sb >= 4 * tb:
            t.wait_ge(s_dv.sem, R[("mask", h, tb, sb)])
        t.matmul(po[:, col0:512],
                 v_sb[:, sb * 256 + h * 128: sb * 256 + (h + 1) * 128],
                 p_sb[:, base + col0: base + 512],
                 start=(sb == 0), stop=(sb == nsb - 1))
        ist = t.matmul(pbC[0:1, col0:512], ones_col_b[:, :],
                       p_sb[:, base + col0: base + 512],
                       start=(sb == 0), stop=(sb == nsb - 1))
        R[("avd", h, tb, sb)] = s_pe.inc(ist)

    @block.tensor
    def _(t):
        t.wait_ge(s_ac.sem, R[("qk_evac", tb)])
        t.wait_ge(s_dv.sem, R[("v_evac_tb", tb, 3)])
        kd = R.get(("dinv_prev", h, tb))
        if kd is not None:
            t.wait_ge(s_ac.sem, kd)
        mmS(t, 0)
        mmS(t, 1)

    for k in range(npair):
        kpar = k % 2

        @block.scalar
        def _(sc, k=k, kpar=kpar):
            if k >= 2:
                sc.wait_ge(s_pe.sem, R[("avd", h, tb, 2 * (k - 2) + 1)])
            else:
                kp = R.get(("p_free", h, tb, kpar))
                if kp is not None:
                    sc.wait_ge(s_pe.sem, kp)
            sc.wait_ge(s_pe.sem, R[("mmS", h, tb, 2 * k + 1)])
            col0 = max(0, 2 * k * PART - tb * 512)
            ist = sc.activation(p_sb[:, kpar * 1024 + col0: (kpar + 1) * 1024],
                                wides[kpar][:, col0:1024], AF.Exp, scale=QSCALE)
            R[("exp", h, tb, k)] = s_ac.inc(ist)

        dsbs = [sb for sb in (2 * k, 2 * k + 1) if sb >= 4 * tb]
        if dsbs:
            @block.vector
            def _(v, k=k, kpar=kpar, dsbs=dsbs):
                v.wait_ge(s_ac.sem, R[("exp", h, tb, k)])
                for sb in dsbs:
                    half = sb % 2
                    col0 = sb * PART - tb * 512
                    sl = p_sb[:, kpar * 1024 + half * 512 + col0:
                              kpar * 1024 + half * 512 + col0 + PART]
                    ist = v.tensor_mul(sl, sl, causal_b[:, :])
                    R[("mask", h, tb, sb)] = s_dv.inc(ist)

        @block.tensor
        def _(t, k=k, kpar=kpar, dsbs=dsbs):
            if k + 1 < npair:
                if k >= 1:
                    t.wait_ge(s_ac.sem, R[("exp", h, tb, k - 1)])
                else:
                    kw = R.get(("exp_last_par", h, tb, 1))
                    if kw is not None:
                        t.wait_ge(s_ac.sem, kw)
                mmS(t, 2 * k + 2)
                mmS(t, 2 * k + 3)
            if not dsbs:
                t.wait_ge(s_ac.sem, R[("exp", h, tb, k)])
            avd(t, 2 * k)
            avd(t, 2 * k + 1)

    R[("exp_last", h, tb)] = R[("exp", h, tb, npair - 1)]
    R[("exp_last_par", nh, ntb, (npair - 1) % 2)] = R[("exp", h, tb, npair - 1)]
    if npair >= 2:
        R[("exp_last_par", nh, ntb, (npair - 2) % 2)] = R[("exp", h, tb, npair - 2)]
    else:
        prev = R.get(("exp_last_par", h, tb, 1))
        if prev is not None:
            R[("exp_last_par", nh, ntb, 1)] = prev
    R[("p_free", nh, ntb, (npair - 1) % 2)] = R[("avd", h, tb, nsb - 1)]
    if npair >= 2:
        R[("p_free", nh, ntb, (npair - 2) % 2)] = R[("avd", h, tb, nsb - 3)]

    @block.scalar
    def _(sc):
        sc.wait_ge(s_pe.sem, R[("avd", h, tb, nsb - 1)])
        s_ac.inc(sc.activation(ln_row[0:1, :], pbC[0:1, :], AF.Ln))
        sc.wait_ge(s_ac.sem, s_ac.v)
        ist = sc.activation(dinv_sb[0:1, :], ln_row[0:1, :], AF.Exp, scale=-1.0)
        R[("dinv", h, tb)] = s_ac.inc(ist)
        R[("dinv_prev", nh, ntb)] = s_ac.v

    @block.tensor
    def _(t):
        t.wait_ge(s_ac.sem, R[("dinv", h, tb)])
        kb = R.get(("bc_free", h, tb))
        if kb is not None:
            t.wait_ge(s_dv.sem, kb)
        ist = t.matmul(pbD[:, :], ones_row_f[0:1, :], dinv_sb[0:1, :],
                       start=True, stop=True)
        R[("bc", h, tb)] = s_pe.inc(ist)

    @block.vector
    def _(v):
        v.wait_ge(s_pe.sem, R[("bc", h, tb)])
        s_dv.inc(v.tensor_copy(bc_sb[:, :], pbD[:, :]))
        R[("bc_free", nh, ntb)] = s_dv.v
        ka = R.get(("ao_store_prev", h, tb))
        if ka is not None:
            v.wait_ge(s_ao[h][(tb - 1) % 2].sem, ka)
        v.wait_ge(s_dv.sem, s_dv.v)
        ist = v.tensor_mul(ao_sb[:, h * 512:(h + 1) * 512], po[:, :],
                           bc_sb[:, :])
        R[("ao", h, tb)] = s_dv.inc(ist)

    @block.sync
    def _(s):
        s.wait_ge(s_dv.sem, R[("ao", h, tb)])
        core_j, piece = tb // 2, tb % 2
        ist = s.dma_start(
            out=cc_in[piece][256 * core_j + 128 * h:
                             256 * core_j + 128 * (h + 1), :],
            in_=ao_sb[:, h * 512:(h + 1) * 512])
        s_ao[h][piece].inc(ist, 16)
        R[("ao_store_prev", h, tb + 1)] = s_ao[h][piece].v


def _emit_p4_half(block, R, half, aoh, wo_buf, z_sb, x1bf, bo_sb,
                  zbf_t, sq_t, mean_sb, msq_sb, tmp_row, ln_row, rstd_row,
                  meanbc, rstdbc, epst, ones_col_b, ones_row_f, wide0, pAB,
                  pbC, pbD, cc_out, xres, wo_bf, x1_dram, wbf_v, s_gld,
                  s_cc, s_sld, s_ao, s_x1, s_wo, s_pe, s_ac, s_dv):
    tag = ("ln1", half)

    @block.sync
    def _(s):
        s.wait_ge(s_cc.sem, 1)
        ist = s.dma_start(out=aoh.rearrange("p (c t) -> p c t", t=512),
                          in_=cc_out.rearrange("(c p) t -> p c t", p=PART))
        s_sld.inc(ist, 16)
        ist = s.dma_start(out=z_sb.rearrange("p (c t) -> p c t", t=512),
                          in_=xres[:, half * 512:(half + 1) * 512]
                              .rearrange("(c p) t -> p c t", p=PART))
        s_sld.inc(ist, 16)
        R[("p4_loads", half)] = s_sld.v

    for jc in range(KCH):
        par = jc % 2
        idx = half * KCH + jc

        @block.sync
        def _(s, jc=jc, par=par, idx=idx):
            if idx >= 2:
                s.wait_ge(s_pe.sem, R[("wo_used", idx - 2)])
            elif half == 0 and jc == 0:
                cd = R.get("cc_in_done", ((0, 0), (0, 0)))
                for piece in range(2):
                    for hh in range(HPC):
                        if cd[piece][hh]:
                            s.wait_ge(s_ao[hh][piece].sem, cd[piece][hh])
                s.wait_ge(s_gld.sem, wbf_v)
            ist = s.dma_start(out=wo_buf[par].rearrange("p (c j) -> p c j", j=PART),
                              in_=wo_bf[:, jc * PART:(jc + 1) * PART]
                                  .rearrange("(c p) j -> p c j", p=PART))
            s_wo[par].inc(ist, 16)
            R[("wo_ld", idx)] = s_wo[par].v

        @block.tensor
        def _(t, jc=jc, par=par, idx=idx):
            if jc == 0:
                t.wait_ge(s_sld.sem, R[("p4_loads", half)])
            t.wait_ge(s_wo[par].sem, R[("wo_ld", idx)])
            if jc >= 2:
                t.wait_ge(s_dv.sem, R[("z_evac", half, jc - 2)])
            for hc in range(KCH):
                ist = t.matmul(pAB[par][:, :],
                               wo_buf[par][:, hc * PART:(hc + 1) * PART],
                               aoh[:, hc * 512:(hc + 1) * 512],
                               start=(hc == 0), stop=(hc == KCH - 1))
            R[("wo_mm", half, jc)] = s_pe.inc(ist)
            R[("wo_used", idx)] = s_pe.v
            if jc >= 2:
                _stats_mm(t, R, tag, jc - 2, zbf_t, sq_t, ones_col_b, pbC, pbD,
                          s_ac, s_pe)

        @block.vector
        def _(v, jc=jc, par=par):
            v.wait_ge(s_pe.sem, R[("wo_mm", half, jc)])
            if jc == 0:
                v.wait_ge(s_sld.sem, R[("p4_loads", half)])
            ist = v.scalar_tensor_tensor(z_sb[:, jc * 512:(jc + 1) * 512],
                                         pAB[par][:, :], bo_sb[:, jc:jc + 1],
                                         z_sb[:, jc * 512:(jc + 1) * 512],
                                         ALU.add, ALU.add)
            R[("z_evac", half, jc)] = s_dv.inc(ist)

        @block.scalar
        def _(sc, jc=jc):
            tpar = jc % 2
            sc.wait_ge(s_dv.sem, R[("z_evac", half, jc)])
            if jc >= 2:
                sc.wait_ge(s_pe.sem, R[("stat_mm", tag, jc - 2)])
            ist = sc.activation(zbf_t[:, tpar * 512:(tpar + 1) * 512],
                                z_sb[:, jc * 512:(jc + 1) * 512], AF.Identity)
            s_ac.inc(ist)
            ist = sc.activation(sq_t[:, tpar * 512:(tpar + 1) * 512],
                                z_sb[:, jc * 512:(jc + 1) * 512], AF.Square)
            R[("zbfsq", tag, jc)] = s_ac.inc(ist)

    @block.tensor
    def _(t):
        _stats_mm(t, R, tag, KCH - 2, zbf_t, sq_t, ones_col_b, pbC, pbD, s_ac, s_pe)
        _stats_mm(t, R, tag, KCH - 1, zbf_t, sq_t, ones_col_b, pbC, pbD, s_ac, s_pe)
        R[("wo_mm_done", half)] = s_pe.v

    _ln_finish(block, R, tag, z_sb, mean_sb, msq_sb, tmp_row, ln_row, rstd_row,
               meanbc, rstdbc, epst, ones_row_f, wide0, pbC, pbD, s_pe, s_ac, s_dv,
               bc_mean=(wide0[:, 0:512] if half == 0 else pAB[0][:, :]),
               bc_rstd=(wide0[:, 512:1024] if half == 0 else pAB[1][:, :]))

    @block.scalar
    def _(sc):
        for jc in range(KCH):
            sc.wait_ge(s_dv.sem, R[("xnorm", tag, jc)])
            ist = sc.activation(
                x1bf[:, jc * TC + half * 512: jc * TC + (half + 1) * 512],
                z_sb[:, jc * 512:(jc + 1) * 512], AF.Identity)
            s_ac.inc(ist)
        R[("x1bf_done", half)] = s_ac.v

    @block.sync
    def _(s):
        s.wait_ge(s_dv.sem, R[("ln_done", tag)])
        ist = s.dma_start(out=x1_dram[:, half * 512:(half + 1) * 512]
                          .rearrange("(c p) t -> p c t", p=PART),
                          in_=z_sb.rearrange("p (c t) -> p c t", t=512))
        s_x1[half].inc(ist, 16)
        R[("x1_store", half)] = s_x1[half].v


def _stats_mm(t, R, tag, jc, zbf_t, sq_t, ones_col_b, pbC, pbD, s_ac, s_pe):
    tpar = jc % 2
    t.wait_ge(s_ac.sem, R[("zbfsq", tag, jc)])
    t.matmul(pbC[0:1, :], ones_col_b[:, :], zbf_t[:, tpar * 512:(tpar + 1) * 512],
             start=(jc == 0), stop=(jc == KCH - 1))
    ist = t.matmul(pbD[0:1, :], ones_col_b[:, :], sq_t[:, tpar * 512:(tpar + 1) * 512],
                   start=(jc == 0), stop=(jc == KCH - 1))
    R[("stat_mm", tag, jc)] = s_pe.inc(ist)


def _ln_finish(block, R, tag, zbuf, mean_sb, msq_sb, tmp_row, ln_row, rstd_row,
               meanbc, rstdbc, epst, ones_row_f, widebc, pbC, pbD,
               s_pe, s_ac, s_dv, slice_fn=None, stat_mean=None, stat_sq=None,
               bc_mean=None, bc_rstd=None):
    if slice_fn is None:
        slice_fn = lambda jc: zbuf[:, jc * 512:(jc + 1) * 512]
    if stat_mean is None:
        stat_mean = pbC[0:1, :]
    if stat_sq is None:
        stat_sq = pbD[0:1, :]
    if bc_mean is None:
        bc_mean = widebc[:, 0:512]
    if bc_rstd is None:
        bc_rstd = widebc[:, 512:1024]

    @block.vector
    def _(v):
        v.wait_ge(s_pe.sem, R[("stat_mm", tag, KCH - 1)])
        s_dv.inc(v.tensor_scalar(mean_sb[0:1, :], stat_mean, 1.0 / D, None, ALU.mult))
        s_dv.inc(v.tensor_scalar(msq_sb[0:1, :], stat_sq, 1.0 / D, None, ALU.mult))
        R[("rows_done", tag)] = s_dv.v
        v.wait_ge(s_dv.sem, s_dv.v)
        s_dv.inc(v.tensor_mul(tmp_row[0:1, :], mean_sb[0:1, :], mean_sb[0:1, :]))
        v.wait_ge(s_dv.sem, s_dv.v)
        ist = v.tensor_sub(tmp_row[0:1, :], msq_sb[0:1, :], tmp_row[0:1, :])
        R[("var", tag)] = s_dv.inc(ist)

    @block.scalar
    def _(sc):
        sc.wait_ge(s_dv.sem, R[("var", tag)])
        s_ac.inc(sc.activation(ln_row[0:1, :], tmp_row[0:1, :], AF.Ln,
                               bias=epst[0:1, 0:1]))
        sc.wait_ge(s_ac.sem, s_ac.v)
        ist = sc.activation(rstd_row[0:1, :], ln_row[0:1, :], AF.Exp, scale=-0.5)
        R[("rstd", tag)] = s_ac.inc(ist)

    @block.tensor
    def _(t):
        t.wait_ge(s_ac.sem, R[("rstd", tag)])
        kx = R.get(("bc_extra_dv", tag))
        if kx is not None:
            t.wait_ge(s_dv.sem, kx)
        t.matmul(bc_mean, ones_row_f[0:1, :], mean_sb[0:1, :],
                 start=True, stop=True)
        ist = t.matmul(bc_rstd, ones_row_f[0:1, :], rstd_row[0:1, :],
                       start=True, stop=True)
        R[("bcast", tag)] = s_pe.inc(ist)

    @block.vector
    def _(v):
        v.wait_ge(s_pe.sem, R[("bcast", tag)])
        s_dv.inc(v.tensor_copy(meanbc[:, :], bc_mean))
        s_dv.inc(v.tensor_copy(rstdbc[:, :], bc_rstd))
        v.wait_ge(s_dv.sem, s_dv.v)
        for jc in range(KCH):
            sl = slice_fn(jc)
            s_dv.inc(v.tensor_sub(sl, sl, meanbc[:, :]))
            v.wait_ge(s_dv.sem, s_dv.v)
            ist = v.tensor_mul(sl, sl, rstdbc[:, :])
            R[("xnorm", tag, jc)] = s_dv.inc(ist)
        R[("ln_done", tag)] = s_dv.v


def _emit_p5a(block, R, x1bf, wg_buf, wu_buf, sg_t, h_t, A_t,
              pM01, pm01, pM23, pm23, thr_t, mask_t, hsp_t,
              wide0, wide1, wgate, wup, h_dram,
              s_wgu, s_hs, s_x1, s_pe, s_ac, s_dv):
    NJC = INTER // 512
    NTB8 = TC // PART

    for jc in range(NJC):
        par = jc % 2

        @block.gpsimd
        def _(g, jc=jc, par=par):
            if jc < 2:
                g.wait_ge(s_pe.sem, R[("wo_mm_done", 1)])
                g.wait_ge(s_x1[0].sem, R[("x1_store", 0)])
                g.wait_ge(s_ac.sem, R[("x1bf_done", 0)])
            else:
                g.wait_ge(s_pe.sem, R[("gu_used", jc - 2)])
            ist = g.dma_start(out=wg_buf[par].rearrange("p (c j) -> p c j", j=512),
                              in_=wgate[:, jc * 512:(jc + 1) * 512]
                                  .rearrange("(c p) j -> p c j", p=PART))
            s_wgu[par].inc(ist, 16)
            ist = g.dma_start(out=wu_buf[par].rearrange("p (c j) -> p c j", j=512),
                              in_=wup[:, jc * 512:(jc + 1) * 512]
                                  .rearrange("(c p) j -> p c j", p=PART))
            s_wgu[par].inc(ist, 16)
            R[("gu_ld", jc)] = s_wgu[par].v

        for tb8 in range(NTB8):
            it = jc * NTB8 + tb8
            ppar = it % 2

            @block.tensor
            def _(t, jc=jc, par=par, tb8=tb8, it=it, ppar=ppar):
                if tb8 == 0:
                    t.wait_ge(s_wgu[par].sem, R[("gu_ld", jc)])
                if tb8 == 0 and jc == 0:
                    t.wait_ge(s_ac.sem, R[("x1bf_done", 0)])
                if tb8 == 4 and jc == 0:
                    t.wait_ge(s_ac.sem, R[("x1bf_done", 1)])
                if it >= 2:
                    t.wait_ge(s_ac.sem, R[("silu", it - 2)])
                    t.wait_ge(s_dv.sem, R[("hmul", it - 2)])
                for kc in range(KCH):
                    lhs = x1bf[:, kc * TC + tb8 * PART: kc * TC + (tb8 + 1) * PART]
                    t.matmul(wide0[:, ppar * 512:(ppar + 1) * 512], lhs,
                             wg_buf[par][:, kc * 512:(kc + 1) * 512],
                             start=(kc == 0), stop=(kc == KCH - 1))
                    ist = t.matmul(wide1[:, ppar * 512:(ppar + 1) * 512], lhs,
                                   wu_buf[par][:, kc * 512:(kc + 1) * 512],
                                   start=(kc == 0), stop=(kc == KCH - 1))
                R[("gu_mm", it)] = s_pe.inc(ist)
                if tb8 == NTB8 - 1:
                    R[("gu_used", jc)] = s_pe.v

            @block.scalar
            def _(sc, it=it, ppar=ppar):
                sc.wait_ge(s_pe.sem, R[("gu_mm", it)])
                if it >= 2:
                    sc.wait_ge(s_dv.sem, R[("hmul", it - 2)])
                ist = sc.activation(sg_t[:, ppar * 512:(ppar + 1) * 512],
                                    wide0[:, ppar * 512:(ppar + 1) * 512], AF.Silu)
                R[("silu", it)] = s_ac.inc(ist)

            @block.vector
            def _(v, jc=jc, tb8=tb8, it=it, ppar=ppar):
                v.wait_ge(s_ac.sem, R[("silu", it)])
                v.wait_ge(s_dv.sem, s_dv.v)
                ist = v.tensor_tensor(h_t[:, :], sg_t[:, ppar * 512:(ppar + 1) * 512],
                                      wide1[:, ppar * 512:(ppar + 1) * 512], ALU.mult)
                R[("hmul", it)] = s_dv.inc(ist)
                v.wait_ge(s_dv.sem, s_dv.v)
                s_dv.inc(v.scalar_tensor_tensor(A_t[:, :], h_t[:, :], -1.0,
                                                h_t[:, :], ALU.mult, ALU.max))
                v.wait_ge(s_dv.sem, s_dv.v)
                A4 = A_t.ap().rearrange("p (g i) -> p g i", i=4)
                s_dv.inc(v.tensor_tensor(pM01[:, :], A4[:, :, 0], A4[:, :, 1], ALU.max))
                s_dv.inc(v.tensor_tensor(pm01[:, :], A4[:, :, 0], A4[:, :, 1], ALU.min))
                s_dv.inc(v.tensor_tensor(pM23[:, :], A4[:, :, 2], A4[:, :, 3], ALU.max))
                s_dv.inc(v.tensor_tensor(pm23[:, :], A4[:, :, 2], A4[:, :, 3], ALU.min))
                v.wait_ge(s_dv.sem, s_dv.v)
                s_dv.inc(v.tensor_tensor(pM01[:, :], pM01[:, :], pM23[:, :], ALU.min))
                s_dv.inc(v.tensor_tensor(pm01[:, :], pm01[:, :], pm23[:, :], ALU.max))
                v.wait_ge(s_dv.sem, s_dv.v)
                s_dv.inc(v.tensor_tensor(thr_t[:, :], pM01[:, :], pm01[:, :], ALU.max))
                v.wait_ge(s_dv.sem, s_dv.v)
                M4 = mask_t.ap().rearrange("p (g i) -> p g i", i=4)
                for i in range(4):
                    s_dv.inc(v.tensor_tensor(M4[:, :, i], A4[:, :, i], thr_t[:, :],
                                             ALU.is_ge))
                v.wait_ge(s_dv.sem, s_dv.v)
                ks = R.get(("hsp_store", it - 2))
                if ks is not None:
                    v.wait_ge(s_hs[it % 2].sem, ks)
                ist = v.tensor_tensor(hsp_t[:, ppar * 512:(ppar + 1) * 512],
                                      h_t[:, :], mask_t[:, :], ALU.mult)
                R[("hsp", it)] = s_dv.inc(ist)

            @block.sync
            def _(s, jc=jc, tb8=tb8, it=it, ppar=ppar):
                s.wait_ge(s_dv.sem, R[("hsp", it)])
                ist = s.dma_start(out=h_dram[tb8 * PART:(tb8 + 1) * PART,
                                             jc * 512:(jc + 1) * 512],
                                  in_=hsp_t[:, ppar * 512:(ppar + 1) * 512])
                s_hs[ppar].inc(ist, 16)
                R[("hsp_store", it)] = s_hs[ppar].v

    R["h_all_stored"] = (s_hs[0].v, s_hs[1].v)
    R["gu_all_used"] = s_pe.v


def _emit_p5b(block, R, hfm_q, wd_buf, z2_sb,
              zbf_t, sq_t, mean_sb, msq_sb, tmp_row, ln_row, rstd_row,
              meanbc, rstdbc, epst, ones_col_b, ones_row_f,
              wide0, wide1, pAB, pbC, pbD, wd_bf, h_dram, x1_dram, wbf_v, s_gld,
              s_wd, s_hT, s_sld, s_hs, s_x1, s_pe, s_ac, s_dv):
    _DOWN_PSUM = [pAB[0], pAB[1], wide0[:, 0:512], wide0[:, 512:1024]]

    @block.scalar
    def _(sc):
        sc.wait_ge(s_pe.sem, R["gu_all_used"])
        sc.wait_ge(s_x1[0].sem, R[("x1_store", 0)])
        sc.wait_ge(s_x1[1].sem, R[("x1_store", 1)])
        ist = sc.dma_start(out=z2_sb.rearrange("p (c t) -> p c t", t=TC),
                          in_=x1_dram.rearrange("(c p) t -> p c t", p=PART))
        s_sld.inc(ist, 16)
        R["z2_init"] = s_sld.v

    for ipart in range(4):
        hpar = ipart % 2

        @block.sync
        def _(s, ipart=ipart, hpar=hpar):
            if ipart == 0:
                hs0, hs1 = R["h_all_stored"]
                if hs0:
                    s.wait_ge(s_hs[0].sem, hs0)
                if hs1:
                    s.wait_ge(s_hs[1].sem, hs1)
            if ipart >= 2:
                s.wait_ge(s_pe.sem, R[("down_pe", ipart - 2)])
            for i16 in range(16):
                ic = ipart * 16 + i16
                ist = s.dma_start_transpose(
                    hfm_q[hpar][:, i16 * 1024:(i16 + 1) * 1024],
                    h_dram[:, ic * PART:(ic + 1) * PART])
                s_hT[hpar].inc(ist, 16)
            R[("hT", ipart)] = s_hT[hpar].v

        for j2 in range(KCH):
            wpar = j2 % 2
            idx = ipart * KCH + j2

            @block.gpsimd
            def _(g, ipart=ipart, j2=j2, wpar=wpar, idx=idx):
                if idx >= 2:
                    g.wait_ge(s_pe.sem, R[("wd_used", idx - 2)])
                elif idx == 0:
                    g.wait_ge(s_pe.sem, R["gu_all_used"])
                    g.wait_ge(s_gld.sem, wbf_v)
                ist = g.dma_start(
                    out=wd_buf[wpar].rearrange("p (c j) -> p c j", j=PART),
                    in_=wd_bf[ipart * 2048:(ipart + 1) * 2048,
                              j2 * PART:(j2 + 1) * PART]
                        .rearrange("(c p) j -> p c j", p=PART))
                s_wd[wpar].inc(ist, 16)
                R[("wd_ld", idx)] = s_wd[wpar].v

            @block.tensor
            def _(t, ipart=ipart, j2=j2, wpar=wpar, idx=idx, hpar=hpar):
                if j2 == 0:
                    t.wait_ge(s_hT[hpar].sem, R[("hT", ipart)])
                t.wait_ge(s_wd[wpar].sem, R[("wd_ld", idx)])
                for th in range(2):
                    pos = j2 * 2 + th
                    dtile = _DOWN_PSUM[pos % 4]
                    kz = R.get(("z2_add", ipart, pos - 4))
                    if kz is None and ipart >= 1:
                        kz = R.get(("z2_add", ipart - 1, 28 + pos))
                    if kz is not None:
                        t.wait_ge(s_dv.sem, kz)
                    for c in range(16):
                        ist = t.matmul(dtile[:, :],
                                       wd_buf[wpar][:, c * PART:(c + 1) * PART],
                                       hfm_q[hpar][:, c * 1024 + th * 512:
                                                   c * 1024 + (th + 1) * 512],
                                       start=(c == 0), stop=(c == 15))
                    R[("down_mm", ipart, pos)] = s_pe.inc(ist)
                R[("wd_used", idx)] = s_pe.v

            @block.vector
            def _(v, ipart=ipart, j2=j2):
                if ipart == 0 and j2 == 0:
                    v.wait_ge(s_sld.sem, R["z2_init"])
                for th in range(2):
                    pos = j2 * 2 + th
                    dtile = _DOWN_PSUM[pos % 4]
                    v.wait_ge(s_pe.sem, R[("down_mm", ipart, pos)])
                    sl = z2_sb[:, j2 * TC + th * 512: j2 * TC + (th + 1) * 512]
                    ist = v.tensor_add(sl, sl, dtile[:, :])
                    R[("z2_add", ipart, pos)] = s_dv.inc(ist)

            if ipart == 3:
                @block.scalar
                def _(sc, j2=j2):
                    for th in range(2):
                        tag = ("ln2", th)
                        sc.wait_ge(s_dv.sem, R[("z2_add", 3, j2 * 2 + th)])
                        if j2 >= 1:
                            sc.wait_ge(s_pe.sem, R[("stat_mm", tag, j2 - 1)])
                        sl = z2_sb[:, j2 * TC + th * 512: j2 * TC + (th + 1) * 512]
                        ist = sc.activation(zbf_t[:, th * 512:(th + 1) * 512],
                                            sl, AF.Identity)
                        s_ac.inc(ist)
                        ist = sc.activation(sq_t[:, th * 512:(th + 1) * 512],
                                            sl, AF.Square)
                        R[("zbfsq", tag, j2)] = s_ac.inc(ist)

                @block.tensor
                def _(t, j2=j2):
                    for th in range(2):
                        tag = ("ln2", th)
                        mrow = pbC[0:1, :] if th == 0 else wide1[0:1, 0:512]
                        qrow = pbD[0:1, :] if th == 0 else wide1[0:1, 512:1024]
                        t.wait_ge(s_ac.sem, R[("zbfsq", tag, j2)])
                        t.matmul(mrow, ones_col_b[:, :],
                                 zbf_t[:, th * 512:(th + 1) * 512],
                                 start=(j2 == 0), stop=(j2 == KCH - 1))
                        ist = t.matmul(qrow, ones_col_b[:, :],
                                       sq_t[:, th * 512:(th + 1) * 512],
                                       start=(j2 == 0), stop=(j2 == KCH - 1))
                        R[("stat_mm", tag, j2)] = s_pe.inc(ist)

        @block.tensor
        def _(t, ipart=ipart):
            R[("down_pe", ipart)] = s_pe.v

    R[("bc_extra_dv", ("ln2", 0))] = R[("z2_add", 3, 31)]
    for th in range(2):
        _ln_finish(block, R, ("ln2", th), None,
                   mean_sb, msq_sb, tmp_row, ln_row, rstd_row,
                   meanbc, rstdbc, epst, ones_row_f, wide0, pbC, pbD,
                   s_pe, s_ac, s_dv,
                   slice_fn=lambda jc, th=th: z2_sb[:, jc * TC + th * 512:
                                                    jc * TC + (th + 1) * 512],
                   stat_mean=(pbC[0:1, :] if th == 0 else wide1[0:1, 0:512]),
                   stat_sq=(pbD[0:1, :] if th == 0 else wide1[0:1, 512:1024]),
                   bc_mean=(wide0[:, 0:512] if th == 0 else pbC[:, :]),
                   bc_rstd=(wide0[:, 512:1024] if th == 0 else pbD[:, :]))
        R[("ln2_done", th)] = R[("ln_done", ("ln2", th))]


def _emit_p6_half(block, R, half, z2_sb, panel, identf, wide1, y_out,
                  s_pe, s_dv, s_ot):
    for t128 in range(4):
        pidx = half * 4 + t128
        pi = pidx % 2

        for jc in range(KCH):
            ti = jc % 2

            @block.tensor
            def _(t, t128=t128, jc=jc, ti=ti):
                t.wait_ge(s_dv.sem, R[("xnorm", ("ln2", half), jc)])
                if t128 == 0 and jc == 0 and half == 0:
                    t.wait_ge(s_dv.sem, R[("rows_done", ("ln2", 1))])
                idx = t128 * KCH + jc
                ko = R.get(("outt_copied", half, idx - 2))
                if ko is None and half == 1:
                    ko = R.get(("outt_copied", 0, 4 * KCH + idx - 2))
                if ko is not None:
                    t.wait_ge(s_dv.sem, ko)
                ist = t.transpose(
                    wide1[:, ti * 512: ti * 512 + 128],
                    z2_sb[:, jc * TC + half * 512 + t128 * PART:
                          jc * TC + half * 512 + (t128 + 1) * PART],
                    identf[:, :])
                R[("tr", half, t128, jc)] = s_pe.inc(ist)

            @block.vector
            def _(v, t128=t128, jc=jc, ti=ti, pi=pi, pidx=pidx):
                v.wait_ge(s_pe.sem, R[("tr", half, t128, jc)])
                if jc == 0:
                    ks = R.get(("panel_stored", pidx - 2))
                    if ks is not None:
                        v.wait_ge(s_ot[pi].sem, ks)
                idx = t128 * KCH + jc
                ist = v.tensor_copy(panel[pi][:, jc * PART:(jc + 1) * PART],
                                    wide1[:, ti * 512: ti * 512 + 128])
                R[("outt_copied", half, idx)] = s_dv.inc(ist)

        @block.sync
        def _(s, t128=t128, pi=pi, pidx=pidx):
            s.wait_ge(s_dv.sem, R[("outt_copied", half, t128 * KCH + KCH - 1)])
            trow = half * 512 + t128 * PART
            ist = s.dma_start(out=y_out[trow:trow + PART, :], in_=panel[pi][:, :])
            s_ot[pi].inc(ist, 16)
            R[("panel_stored", pidx)] = s_ot[pi].v
    R[("out_done", half)] = (s_ot[0].v, s_ot[1].v)



_NC_CACHE = {}


def _get_nc():
    if "full" not in _NC_CACHE:
        _NC_CACHE["full"] = build_nc()
    return _NC_CACHE["full"]


def make_in_maps(L0_attn_x, L0_attn_Wqkv, L0_attn_bqkv, L0_attn_Wo, L0_attn_bo,
                 L0_ffn_w_gate, L0_ffn_w_up, L0_ffn_w_down):
    x = np.ascontiguousarray(np.asarray(L0_attn_x, dtype=np.float32).reshape(S, D).T)
    Wqkv = np.asarray(L0_attn_Wqkv, dtype=np.float32)
    bqkv = np.asarray(L0_attn_bqkv, dtype=np.float32)
    Wo = np.ascontiguousarray(np.asarray(L0_attn_Wo, dtype=np.float32))
    bo = np.ascontiguousarray(np.asarray(L0_attn_bo, dtype=np.float32))
    wg = np.ascontiguousarray(np.asarray(L0_ffn_w_gate, dtype=np.float32))
    wu = np.ascontiguousarray(np.asarray(L0_ffn_w_up, dtype=np.float32))
    wd = np.ascontiguousarray(np.asarray(L0_ffn_w_down, dtype=np.float32))
    in_maps = []
    for c in range(NCORE):
        cs = 256 * c
        in_maps.append({
            "xT": x,
            "wq": np.ascontiguousarray(Wqkv[:, cs:cs + 256]),
            "wk": np.ascontiguousarray(Wqkv[:, 2048 + cs:2048 + cs + 256]),
            "wv": np.ascontiguousarray(Wqkv[:, 4096 + cs:4096 + cs + 256]),
            "bq": np.ascontiguousarray(bqkv[cs:cs + 256]),
            "bk": np.ascontiguousarray(bqkv[2048 + cs:2048 + cs + 256]),
            "bv": np.ascontiguousarray(bqkv[4096 + cs:4096 + cs + 256]),
            "wo": Wo,
            "bo": bo,
            "xres": np.ascontiguousarray(x[:, TC * c:TC * (c + 1)]),
            "wgate": wg,
            "wup": wu,
            "wdown": wd,
        })
    return in_maps


def kernel(**inputs):
    in_maps = make_in_maps(**inputs)
    nc = _get_nc()
    res = run_bass_kernel_spmd(nc, in_maps, core_ids=list(range(NCORE)))
    y = np.concatenate([np.asarray(res.results[c]["y"], dtype=np.float32)
                        for c in range(NCORE)], axis=0)
    return y.reshape(1, S, D)


# version 11
# speedup vs baseline: 1.0275x; 1.0102x over previous
import os
os.environ.setdefault("TRN_ALLOW_ICI_ALL_TO_ALL", "1")

import numpy as np
import concourse.bass as bass
import concourse.mybir as mybir
from concourse.bass_utils import run_bass_kernel_spmd

F32 = mybir.dt.float32
BF16 = mybir.dt.bfloat16
AF = mybir.ActivationFunctionType
ALU = mybir.AluOpType

D = 2048
S = 8192
NH = 16
HD = 128
NCORE = 8
HPC = 2
TC = 1024
INTER = 8192
KCH = D // 128
TCHUNK = 512
NTCH = S // TCHUNK
NTB = S // 512
QSCALE = 1.0 / float(np.sqrt(HD))
EPS = 1e-5
PART = 128



class _RecInst:
    __slots__ = ("incs",)

    def __init__(self):
        self.incs = []

    def then_inc(self, sem, n=None):
        self.incs.append((sem, n))
        return self


class _RecEngine:
    def __init__(self, log):
        self._log = log

    def __getattr__(self, meth):
        log = self._log

        def call(*a, **k):
            rec = _RecInst()
            log.append((meth, a, k, rec))
            return rec
        return call


class RecBlock:

    def __init__(self):
        self.logs = {e: [] for e in
                     ("tensor", "scalar", "vector", "gpsimd", "sync")}

    def _dec(self, engine):
        def deco(fn):
            fn(_RecEngine(self.logs[engine]))
        return deco

    @property
    def tensor(self):
        return self._dec("tensor")

    @property
    def scalar(self):
        return self._dec("scalar")

    @property
    def vector(self):
        return self._dec("vector")

    @property
    def gpsimd(self):
        return self._dec("gpsimd")

    @property
    def sync(self):
        return self._dec("sync")

    def replay(self, nc):
        with nc.Block() as blk:
            for engine in ("tensor", "scalar", "vector", "gpsimd", "sync"):
                log = self.logs[engine]

                def emit(eng, log=log):
                    for meth, a, k, rec in log:
                        inst = getattr(eng, meth)(*a, **k)
                        for sem, n in rec.incs:
                            if n is None:
                                inst.then_inc(sem)
                            else:
                                inst.then_inc(sem, n)
                getattr(blk, engine)(emit)


class Ctr:
    def __init__(self, nc, name):
        self.sem = nc.alloc_semaphore(name)
        self.v = 0

    def inc(self, inst, n=1):
        inst.then_inc(self.sem, n)
        self.v += n
        return self.v

    def inc_default(self, inst):
        inst.then_inc(self.sem)
        self.v = max(self.v, 1)
        return 1


def build_nc(phases=("p1", "p2", "p3", "p4", "p5", "p6")):
    nc = bass.Bass("TRN2", target_bir_lowering=False, debug=False)

    xT = nc.dram_tensor("xT", [D, S], F32, kind="ExternalInput").ap()
    wq = nc.dram_tensor("wq", [D, HPC * HD], F32, kind="ExternalInput").ap()
    wk = nc.dram_tensor("wk", [D, HPC * HD], F32, kind="ExternalInput").ap()
    wv = nc.dram_tensor("wv", [D, HPC * HD], F32, kind="ExternalInput").ap()
    bq = nc.dram_tensor("bq", [HPC * HD], F32, kind="ExternalInput").ap()
    bk = nc.dram_tensor("bk", [HPC * HD], F32, kind="ExternalInput").ap()
    bv = nc.dram_tensor("bv", [HPC * HD], F32, kind="ExternalInput").ap()
    wo = nc.dram_tensor("wo", [D, D], F32, kind="ExternalInput").ap()
    bo = nc.dram_tensor("bo", [D], F32, kind="ExternalInput").ap()
    xres = nc.dram_tensor("xres", [D, TC], F32, kind="ExternalInput").ap()
    wgate = nc.dram_tensor("wgate", [D, INTER], F32, kind="ExternalInput").ap()
    wup = nc.dram_tensor("wup", [D, INTER], F32, kind="ExternalInput").ap()
    wdown = nc.dram_tensor("wdown", [INTER, D], F32, kind="ExternalInput").ap()
    y_out = nc.dram_tensor("y", [TC, D], F32, kind="ExternalOutput").ap()

    cc_in = [nc.dram_tensor("cc_in0", [D, 512], BF16),
             nc.dram_tensor("cc_in1", [D, 512], BF16)]
    cc_out = [nc.dram_tensor("cc_out0", [D, 512], BF16),
              nc.dram_tensor("cc_out1", [D, 512], BF16)]
    h_dram = nc.dram_tensor("h_dram", [TC, INTER], BF16)
    x1_dram = nc.dram_tensor("x1_dram", [D, TC], F32)
    wo_bf = nc.dram_tensor("wo_bf", [D, D], BF16)
    wd_bf = nc.dram_tensor("wd_bf", [INTER, D], BF16)

    AR_BYTES = 152 * 1024
    arena = nc.alloc_sbuf_tensor("arena", [PART, AR_BYTES // 2], BF16)
    ar_bf = arena.ap()
    ar_f32 = arena.ap().bitcast(F32)

    def carve_bf(off_kb, cols):
        c0 = off_kb * 1024 // 2
        return ar_bf[:, c0:c0 + cols]

    def carve_f32(off_kb, cols):
        c0 = off_kb * 1024 // 4
        return ar_f32[:, c0:c0 + cols]

    q_sb = carve_bf(0, HPC * S)
    k_sb = carve_bf(32, HPC * S)
    v_sb = carve_bf(64, 64 * 256)
    xbuf = [carve_bf(96, KCH * TCHUNK), carve_bf(112, KCH * TCHUNK)]
    wq_sb = carve_bf(128, KCH * 256)
    wk_sb = carve_bf(136, KCH * 256)
    wv_sb = carve_bf(144, KCH * 256)
    aoh = [carve_bf(0, KCH * 512), carve_bf(16, KCH * 512)]
    wo_buf = [carve_bf(48, KCH * 128), carve_bf(52, KCH * 128)]
    z_sb = [carve_f32(56, KCH * 512), carve_f32(88, KCH * 512)]
    x1bf = carve_bf(120, KCH * TC)
    wg_buf = [carve_bf(0, KCH * 512), carve_bf(16, KCH * 512)]
    wu_buf = [carve_bf(32, KCH * 512), carve_bf(48, KCH * 512)]
    hfm_q = [carve_bf(0, 16 * 1024), carve_bf(32, 16 * 1024)]
    wd_buf = [carve_bf(64, 16 * 128), carve_bf(68, 16 * 128)]
    z2_sb = carve_f32(80, KCH * TC)
    panel = [carve_f32(0, D), carve_f32(8, D)]

    ones_col_b = nc.alloc_sbuf_tensor("ones_col_b", [PART, 1], BF16)
    ones_row_b = nc.alloc_sbuf_tensor("ones_row_b", [1, PART], BF16)
    ones_row_f = nc.alloc_sbuf_tensor("ones_row_f", [1, PART], F32)
    identf = nc.alloc_sbuf_tensor("identf", [PART, PART], F32)
    causal_b = nc.alloc_sbuf_tensor("causal_b", [PART, PART], BF16)
    epst = nc.alloc_sbuf_tensor("epst", [1, 1], F32)
    bq_sb = nc.alloc_sbuf_tensor("bq_sb", [PART, HPC], F32)
    bk_sb = nc.alloc_sbuf_tensor("bk_sb", [PART, HPC], F32)
    bv_row = nc.alloc_sbuf_tensor("bv_row", [1, HPC * HD], BF16)
    bo_sb = nc.alloc_sbuf_tensor("bo_sb", [PART, KCH], F32)

    p_sb = nc.alloc_sbuf_tensor("p_sb", [PART, 2 * 1024], BF16)
    ao_sb = nc.alloc_sbuf_tensor("ao_sb", [PART, 2 * 512], BF16)
    dinv_sb = nc.alloc_sbuf_tensor("dinv_sb", [1, 512], F32)
    bc_sb = nc.alloc_sbuf_tensor("bc_sb", [PART, 512], F32)

    mean_sb = nc.alloc_sbuf_tensor("mean_sb", [1, 512], F32)
    msq_sb = nc.alloc_sbuf_tensor("msq_sb", [1, 512], F32)
    tmp_row = nc.alloc_sbuf_tensor("tmp_row", [1, 512], F32)
    ln_row = nc.alloc_sbuf_tensor("ln_row", [1, 512], F32)
    rstd_row = nc.alloc_sbuf_tensor("rstd_row", [1, 512], F32)
    meanbc = nc.alloc_sbuf_tensor("meanbc", [PART, 512], F32)
    rstdbc = nc.alloc_sbuf_tensor("rstdbc", [PART, 512], F32)
    zbf_t = nc.alloc_sbuf_tensor("zbf_t", [PART, 2 * 512], BF16)
    sq_t = nc.alloc_sbuf_tensor("sq_t", [PART, 2 * 512], BF16)

    sg_t = nc.alloc_sbuf_tensor("sg_t", [PART, 2 * 512], F32)
    h_t = nc.alloc_sbuf_tensor("h_t", [PART, 512], F32)
    A_t = nc.alloc_sbuf_tensor("A_t", [PART, 512], F32)
    pM01 = nc.alloc_sbuf_tensor("pM01", [PART, 128], F32)
    pm01 = nc.alloc_sbuf_tensor("pm01", [PART, 128], F32)
    pM23 = nc.alloc_sbuf_tensor("pM23", [PART, 128], F32)
    pm23 = nc.alloc_sbuf_tensor("pm23", [PART, 128], F32)
    thr_t = nc.alloc_sbuf_tensor("thr_t", [PART, 128], F32)
    mask_t = nc.alloc_sbuf_tensor("mask_t", [PART, 512], BF16)
    hsp_t = nc.alloc_sbuf_tensor("hsp_t", [PART, 2 * 512], BF16)
    outt = nc.alloc_sbuf_tensor("outt", [PART, 2 * 128], F32)

    wide0 = nc.alloc_psum_tensor("wide0", [PART, 1024], F32)
    wide1 = nc.alloc_psum_tensor("wide1", [PART, 1024], F32)
    pbA = nc.alloc_psum_tensor("pbA", [PART, 512], F32)
    pbB = nc.alloc_psum_tensor("pbB", [PART, 512], F32)
    pbC = nc.alloc_psum_tensor("pbC", [PART, 512], F32)
    pbD = nc.alloc_psum_tensor("pbD", [PART, 512], F32)
    wides = [wide0, wide1]
    pAB = [pbA, pbB]

    s_gld = Ctr(nc, "s_gld")
    s_xb = [Ctr(nc, "s_xb0"), Ctr(nc, "s_xb1")]
    s_wo = [Ctr(nc, "s_wo0"), Ctr(nc, "s_wo1")]
    s_wgu = [Ctr(nc, "s_wgu0"), Ctr(nc, "s_wgu1")]
    s_wd = [Ctr(nc, "s_wd0"), Ctr(nc, "s_wd1")]
    s_sld = Ctr(nc, "s_sld")
    s_z2i = Ctr(nc, "s_z2i")
    s_hT = [Ctr(nc, "s_hT0"), Ctr(nc, "s_hT1")]
    s_ao = [[Ctr(nc, "s_ao00"), Ctr(nc, "s_ao01")],
            [Ctr(nc, "s_ao10"), Ctr(nc, "s_ao11")]]
    s_cc2 = Ctr(nc, "s_cc2")
    s_x1 = [Ctr(nc, "s_x1a"), Ctr(nc, "s_x1b")]
    s_hs = [Ctr(nc, "s_hs0"), Ctr(nc, "s_hs1")]
    s_ot = [Ctr(nc, "s_ot0"), Ctr(nc, "s_ot1")]
    s_gc = Ctr(nc, "s_gc")
    s_pe = Ctr(nc, "s_pe")
    s_ac = Ctr(nc, "s_ac")
    s_dv = Ctr(nc, "s_dv")
    s_cc = Ctr(nc, "s_cc")

    R = {}

    if True:
        block = RecBlock()
        @block.gpsimd
        def _(g):
            g.memset(ones_col_b[:, :], 1.0)
            g.memset(ones_row_b[:, :], 1.0)
            g.memset(ones_row_f[:, :], 1.0)
            g.memset(epst[:, :], EPS)
            g.memset(identf[:, :], 1.0)
            ist = g.memset(causal_b[:, :], 1.0)
            s_gc.inc(ist)
            g.wait_ge(s_gc.sem, s_gc.v)
            g.affine_select(identf[:, :], identf[:, :], [[1, PART]], ALU.is_equal,
                            0.0, base=0, channel_multiplier=-1)
            ist = g.affine_select(causal_b[:, :], causal_b[:, :], [[1, PART]], ALU.is_ge,
                                  0.0, base=0, channel_multiplier=-1)
            s_gc.inc(ist)
            s_gld.inc(g.dma_start(out=wq_sb.rearrange("p (c j) -> p c j", j=256),
                                  in_=wq.rearrange("(c p) j -> p c j", p=PART)), 16)
            s_gld.inc(g.dma_start(out=wk_sb.rearrange("p (c j) -> p c j", j=256),
                                  in_=wk.rearrange("(c p) j -> p c j", p=PART)), 16)
            s_gld.inc(g.dma_start(out=wv_sb.rearrange("p (c j) -> p c j", j=256),
                                  in_=wv.rearrange("(c p) j -> p c j", p=PART)), 16)
            for h in range(HPC):
                s_gld.inc(g.dma_start(out=bq_sb[:, h], in_=bq[h * HD:(h + 1) * HD]), 16)
                s_gld.inc(g.dma_start(out=bk_sb[:, h], in_=bk[h * HD:(h + 1) * HD]), 16)
            s_gld.inc(g.dma_start(out=bv_row[0:1, :],
                                  in_=bv.rearrange("(a j) -> a j", a=1)), 16)
            for jc in range(KCH):
                s_gld.inc(g.dma_start(out=bo_sb[:, jc],
                                      in_=bo[jc * PART:(jc + 1) * PART]), 16)
        R["consts_gc"] = s_gc.v
        R["consts_gld"] = s_gld.v

        if "p1" in phases:
            for tc in range(NTCH):
                par = tc % 2

                @block.gpsimd
                def _(g, tc=tc, par=par):
                    if tc >= 2:
                        g.wait_ge(s_pe.sem, R[("xbuf_free", tc - 2)])
                    ist = g.dma_start(
                        out=xbuf[par].rearrange("p (c t) -> p c t", t=TCHUNK),
                        in_=xT[:, tc * TCHUNK:(tc + 1) * TCHUNK]
                            .rearrange("(c p) t -> p c t", p=PART))
                    s_xb[par].inc(ist, 16)
                    R[("xbuf", tc)] = s_xb[par].v

                @block.tensor
                def _(t, tc=tc, par=par):
                    if tc == 0:
                        t.wait_ge(s_gc.sem, R["consts_gc"])
                        t.wait_ge(s_gld.sem, R["consts_gld"])
                    t.wait_ge(s_xb[par].sem, R[("xbuf", tc)])
                    if tc >= 1:
                        t.wait_ge(s_ac.sem, R[("exp_last", 1, tc - 1)])
                    for h in range(HPC):
                        for kc in range(KCH):
                            ist = t.matmul(wide0[:, h * 512:(h + 1) * 512],
                                           wq_sb[:, (kc * 2 + h) * HD:(kc * 2 + h + 1) * HD],
                                           xbuf[par][:, kc * TCHUNK:(kc + 1) * TCHUNK],
                                           start=(kc == 0), stop=(kc == KCH - 1))
                        R[("q_done", tc, h)] = s_pe.inc(ist)
                        for kc in range(KCH):
                            ist = t.matmul(wide1[:, h * 512:(h + 1) * 512],
                                           wk_sb[:, (kc * 2 + h) * HD:(kc * 2 + h + 1) * HD],
                                           xbuf[par][:, kc * TCHUNK:(kc + 1) * TCHUNK],
                                           start=(kc == 0), stop=(kc == KCH - 1))
                        R[("k_done", tc, h)] = s_pe.inc(ist)
                for tb in range(4):
                    @block.tensor
                    def _(t, tc=tc, par=par, tb=tb):
                        vp = pAB[tb % 2]
                        if tb >= 2:
                            t.wait_ge(s_dv.sem, R[("v_evac_tb", tc, tb - 2)])
                        elif tc >= 1:
                            t.wait_ge(s_dv.sem, R[("v_evac_tb", tc - 1, tb + 2)])
                            t.wait_ge(s_dv.sem, R[("ao", tb % 2, tc - 1)])
                        for kc in range(KCH):
                            t.matmul(vp[:, 0:256],
                                     xbuf[par][:, kc * TCHUNK + tb * PART:
                                               kc * TCHUNK + (tb + 1) * PART],
                                     wv_sb[:, kc * 256:(kc + 1) * 256],
                                     start=(kc == 0), stop=False)
                        ist = t.matmul(vp[:, 0:256], ones_row_b[0:1, :],
                                       bv_row[0:1, :], start=False, stop=True)
                        R[("v_done", tc, tb)] = s_pe.inc(ist)
                        if tb == 3:
                            R[("xbuf_free", tc)] = s_pe.v

                    @block.vector
                    def _(v, tc=tc, tb=tb):
                        vp = pAB[tb % 2]
                        v.wait_ge(s_pe.sem, R[("v_done", tc, tb)])
                        tbg = tc * 4 + tb
                        ist = v.tensor_copy(v_sb[:, tbg * 256:(tbg + 1) * 256],
                                            vp[:, 0:256])
                        R[("v_evac_tb", tc, tb)] = s_dv.inc(ist)

                @block.scalar
                def _(sc, tc=tc):
                    for h in range(HPC):
                        sc.wait_ge(s_pe.sem, R[("q_done", tc, h)])
                        ist = sc.activation(
                            q_sb[:, h * S + tc * TCHUNK: h * S + (tc + 1) * TCHUNK],
                            wide0[:, h * 512:(h + 1) * 512], AF.Identity,
                            bias=bq_sb[:, h:h + 1])
                        s_ac.inc(ist)
                        sc.wait_ge(s_pe.sem, R[("k_done", tc, h)])
                        ist = sc.activation(
                            k_sb[:, h * S + tc * TCHUNK: h * S + (tc + 1) * TCHUNK],
                            wide1[:, h * 512:(h + 1) * 512], AF.Identity,
                            bias=bk_sb[:, h:h + 1])
                        s_ac.inc(ist)
                    R[("qk_evac", tc)] = s_ac.v

                if "p2" in phases:
                    for h in range(HPC):
                        _emit_attn_tb(block, R, h, tc, q_sb, k_sb, v_sb, p_sb,
                                      ao_sb, dinv_sb, bc_sb, causal_b,
                                      ones_col_b, ones_row_f,
                                      wides, pAB, pbC, pbD, cc_in,
                                      s_pe, s_ac, s_dv, s_ao, ln_row)
            if "p2" in phases:
                R["cc_in_done"] = ((s_ao[0][0].v, s_ao[1][0].v),
                                   (s_ao[0][1].v, s_ao[1][1].v))

        @block.gpsimd
        def _(g):
            s_gld.inc(g.dma_start(out=wo_bf.ap(), in_=wo), 16)
            for q4 in range(4):
                s_gld.inc(g.dma_start(out=wd_bf[q4 * 2048:(q4 + 1) * 2048, :],
                                      in_=wdown[q4 * 2048:(q4 + 1) * 2048, :]), 16)
        R["wbf_conv"] = s_gld.v

        if "p3" in phases:
            @block.gpsimd
            def _(g):
                cd = R.get("cc_in_done", ((0, 0), (0, 0)))
                for piece, ctr in ((0, s_cc), (1, s_cc2)):
                    for h in range(HPC):
                        if cd[piece][h]:
                            g.wait_ge(s_ao[h][piece].sem, cd[piece][h])
                    ist = g.collective_compute(
                        "AllToAll", ALU.bypass,
                        replica_groups=[list(range(NCORE))],
                        ins=[cc_in[piece].ap().opt()],
                        outs=[cc_out[piece].ap().opt()])
                    ctr.inc_default(ist)

        if "p4" in phases:
            for half in range(2):
                _emit_p4_half(block, R, half, aoh[half], wo_buf, z_sb[half], x1bf,
                              bo_sb, zbf_t, sq_t, mean_sb, msq_sb, tmp_row,
                              ln_row, rstd_row, meanbc, rstdbc, epst,
                              ones_col_b, ones_row_f, wide0, pAB, pbC, pbD,
                              cc_out[half], xres, wo_bf, x1_dram,
                              R["wbf_conv"], s_gld,
                              (s_cc if half == 0 else s_cc2), s_sld, s_ao,
                              s_x1, s_wo, s_pe, s_ac, s_dv)

        if "p5" in phases:
            _emit_p5a(block, R, x1bf, wg_buf, wu_buf, sg_t, h_t, A_t,
                      pM01, pm01, pM23, pm23, thr_t, mask_t, hsp_t,
                      wide0, wide1, wgate, wup, h_dram,
                      s_wgu, s_hs, s_x1, s_pe, s_ac, s_dv)
            _emit_p5b(block, R, hfm_q, wd_buf, z2_sb,
                      zbf_t, sq_t, mean_sb, msq_sb, tmp_row, ln_row,
                      rstd_row, meanbc, rstdbc, epst, ones_col_b,
                      ones_row_f, wide0, wide1, pAB, pbC, pbD,
                      wd_bf, h_dram, x1_dram, R["wbf_conv"], s_gld,
                      s_wd, s_hT, s_sld, s_z2i, s_hs, s_x1, s_pe, s_ac, s_dv)
            if "p6" in phases:
                for half in range(2):
                    _emit_p6_half(block, R, half, z2_sb, panel, identf, wide1,
                                  y_out, s_pe, s_dv, s_ot)

        @block.sync
        def _(s):
            for ctr in (s_ao[0][0], s_ao[0][1], s_ao[1][0], s_ao[1][1],
                        s_x1[0], s_x1[1], s_hs[0], s_hs[1],
                        s_ot[0], s_ot[1]):
                if ctr.v:
                    s.wait_ge(ctr.sem, ctr.v)

    block.replay(nc)
    return nc


def _emit_attn_tb(block, R, h, tb, q_sb, k_sb, v_sb, p_sb, ao_sb, dinv_sb,
                  bc_sb, causal_b, ones_col_b, ones_row_f, wides, pAB, pbC,
                  pbD, cc_in, s_pe, s_ac, s_dv, s_ao, ln_row):
    nsb = 4 * tb + 4
    npair = nsb // 2
    po = pAB[h]
    nh, ntb = (1, tb) if h == 0 else (0, tb + 1)

    def mmS(t, sb):
        spar2 = (sb // 2) % 2
        half = sb % 2
        col0 = max(0, sb * PART - tb * 512)
        ist = t.matmul(wides[spar2][:, half * 512 + col0: (half + 1) * 512],
                       k_sb[:, h * S + sb * PART: h * S + (sb + 1) * PART],
                       q_sb[:, h * S + tb * 512 + col0: h * S + (tb + 1) * 512],
                       start=True, stop=True)
        R[("mmS", h, tb, sb)] = s_pe.inc(ist)

    def avd(t, sb):
        spar2 = (sb // 2) % 2
        half = sb % 2
        col0 = max(0, sb * PART - tb * 512)
        base = spar2 * 1024 + half * 512
        if sb >= 4 * tb:
            t.wait_ge(s_dv.sem, R[("mask", h, tb, sb)])
        t.matmul(po[:, col0:512],
                 v_sb[:, sb * 256 + h * 128: sb * 256 + (h + 1) * 128],
                 p_sb[:, base + col0: base + 512],
                 start=(sb == 0), stop=(sb == nsb - 1))
        ist = t.matmul(pbC[0:1, col0:512], ones_col_b[:, :],
                       p_sb[:, base + col0: base + 512],
                       start=(sb == 0), stop=(sb == nsb - 1))
        R[("avd", h, tb, sb)] = s_pe.inc(ist)

    @block.tensor
    def _(t):
        t.wait_ge(s_ac.sem, R[("qk_evac", tb)])
        t.wait_ge(s_dv.sem, R[("v_evac_tb", tb, 3)])
        kd = R.get(("dinv_prev", h, tb))
        if kd is not None:
            t.wait_ge(s_ac.sem, kd)
        mmS(t, 0)
        mmS(t, 1)

    for k in range(npair):
        kpar = k % 2

        @block.scalar
        def _(sc, k=k, kpar=kpar):
            if k >= 2:
                sc.wait_ge(s_pe.sem, R[("avd", h, tb, 2 * (k - 2) + 1)])
            else:
                kp = R.get(("p_free", h, tb, kpar))
                if kp is not None:
                    sc.wait_ge(s_pe.sem, kp)
            sc.wait_ge(s_pe.sem, R[("mmS", h, tb, 2 * k + 1)])
            col0 = max(0, 2 * k * PART - tb * 512)
            ist = sc.activation(p_sb[:, kpar * 1024 + col0: (kpar + 1) * 1024],
                                wides[kpar][:, col0:1024], AF.Exp, scale=QSCALE)
            R[("exp", h, tb, k)] = s_ac.inc(ist)

        dsbs = [sb for sb in (2 * k, 2 * k + 1) if sb >= 4 * tb]
        if dsbs:
            @block.vector
            def _(v, k=k, kpar=kpar, dsbs=dsbs):
                v.wait_ge(s_ac.sem, R[("exp", h, tb, k)])
                for sb in dsbs:
                    half = sb % 2
                    col0 = sb * PART - tb * 512
                    sl = p_sb[:, kpar * 1024 + half * 512 + col0:
                              kpar * 1024 + half * 512 + col0 + PART]
                    ist = v.tensor_mul(sl, sl, causal_b[:, :])
                    R[("mask", h, tb, sb)] = s_dv.inc(ist)

        @block.tensor
        def _(t, k=k, kpar=kpar, dsbs=dsbs):
            if k + 1 < npair:
                if k >= 1:
                    t.wait_ge(s_ac.sem, R[("exp", h, tb, k - 1)])
                else:
                    kw = R.get(("exp_last_par", h, tb, 1))
                    if kw is not None:
                        t.wait_ge(s_ac.sem, kw)
                mmS(t, 2 * k + 2)
                mmS(t, 2 * k + 3)
            if not dsbs:
                t.wait_ge(s_ac.sem, R[("exp", h, tb, k)])
            avd(t, 2 * k)
            avd(t, 2 * k + 1)

    R[("exp_last", h, tb)] = R[("exp", h, tb, npair - 1)]
    R[("exp_last_par", nh, ntb, (npair - 1) % 2)] = R[("exp", h, tb, npair - 1)]
    if npair >= 2:
        R[("exp_last_par", nh, ntb, (npair - 2) % 2)] = R[("exp", h, tb, npair - 2)]
    else:
        prev = R.get(("exp_last_par", h, tb, 1))
        if prev is not None:
            R[("exp_last_par", nh, ntb, 1)] = prev
    R[("p_free", nh, ntb, (npair - 1) % 2)] = R[("avd", h, tb, nsb - 1)]
    if npair >= 2:
        R[("p_free", nh, ntb, (npair - 2) % 2)] = R[("avd", h, tb, nsb - 3)]

    @block.scalar
    def _(sc):
        sc.wait_ge(s_pe.sem, R[("avd", h, tb, nsb - 1)])
        s_ac.inc(sc.activation(ln_row[0:1, :], pbC[0:1, :], AF.Ln))
        sc.wait_ge(s_ac.sem, s_ac.v)
        ist = sc.activation(dinv_sb[0:1, :], ln_row[0:1, :], AF.Exp, scale=-1.0)
        R[("dinv", h, tb)] = s_ac.inc(ist)
        R[("dinv_prev", nh, ntb)] = s_ac.v

    @block.tensor
    def _(t):
        t.wait_ge(s_ac.sem, R[("dinv", h, tb)])
        kb = R.get(("bc_free", h, tb))
        if kb is not None:
            t.wait_ge(s_dv.sem, kb)
        ist = t.matmul(pbD[:, :], ones_row_f[0:1, :], dinv_sb[0:1, :],
                       start=True, stop=True)
        R[("bc", h, tb)] = s_pe.inc(ist)

    @block.vector
    def _(v):
        v.wait_ge(s_pe.sem, R[("bc", h, tb)])
        s_dv.inc(v.tensor_copy(bc_sb[:, :], pbD[:, :]))
        R[("bc_free", nh, ntb)] = s_dv.v
        ka = R.get(("ao_store_prev", h, tb))
        if ka is not None:
            v.wait_ge(s_ao[h][(tb - 1) % 2].sem, ka)
        v.wait_ge(s_dv.sem, s_dv.v)
        ist = v.tensor_mul(ao_sb[:, h * 512:(h + 1) * 512], po[:, :],
                           bc_sb[:, :])
        R[("ao", h, tb)] = s_dv.inc(ist)

    @block.sync
    def _(s):
        s.wait_ge(s_dv.sem, R[("ao", h, tb)])
        core_j, piece = tb // 2, tb % 2
        ist = s.dma_start(
            out=cc_in[piece][256 * core_j + 128 * h:
                             256 * core_j + 128 * (h + 1), :],
            in_=ao_sb[:, h * 512:(h + 1) * 512])
        s_ao[h][piece].inc(ist, 16)
        R[("ao_store_prev", h, tb + 1)] = s_ao[h][piece].v


def _emit_p4_half(block, R, half, aoh, wo_buf, z_sb, x1bf, bo_sb,
                  zbf_t, sq_t, mean_sb, msq_sb, tmp_row, ln_row, rstd_row,
                  meanbc, rstdbc, epst, ones_col_b, ones_row_f, wide0, pAB,
                  pbC, pbD, cc_out, xres, wo_bf, x1_dram, wbf_v, s_gld,
                  s_cc, s_sld, s_ao, s_x1, s_wo, s_pe, s_ac, s_dv):
    tag = ("ln1", half)

    @block.sync
    def _(s):
        s.wait_ge(s_cc.sem, 1)
        ist = s.dma_start(out=aoh.rearrange("p (c t) -> p c t", t=512),
                          in_=cc_out.rearrange("(c p) t -> p c t", p=PART))
        s_sld.inc(ist, 16)
        ist = s.dma_start(out=z_sb.rearrange("p (c t) -> p c t", t=512),
                          in_=xres[:, half * 512:(half + 1) * 512]
                              .rearrange("(c p) t -> p c t", p=PART))
        s_sld.inc(ist, 16)
        R[("p4_loads", half)] = s_sld.v

    for jc in range(KCH):
        par = jc % 2
        idx = half * KCH + jc

        @block.sync
        def _(s, jc=jc, par=par, idx=idx):
            if idx >= 2:
                s.wait_ge(s_pe.sem, R[("wo_used", idx - 2)])
            elif half == 0 and jc == 0:
                cd = R.get("cc_in_done", ((0, 0), (0, 0)))
                for piece in range(2):
                    for hh in range(HPC):
                        if cd[piece][hh]:
                            s.wait_ge(s_ao[hh][piece].sem, cd[piece][hh])
                s.wait_ge(s_gld.sem, wbf_v)
            ist = s.dma_start(out=wo_buf[par].rearrange("p (c j) -> p c j", j=PART),
                              in_=wo_bf[:, jc * PART:(jc + 1) * PART]
                                  .rearrange("(c p) j -> p c j", p=PART))
            s_wo[par].inc(ist, 16)
            R[("wo_ld", idx)] = s_wo[par].v

        @block.tensor
        def _(t, jc=jc, par=par, idx=idx):
            if jc == 0:
                t.wait_ge(s_sld.sem, R[("p4_loads", half)])
            t.wait_ge(s_wo[par].sem, R[("wo_ld", idx)])
            if jc >= 2:
                t.wait_ge(s_dv.sem, R[("z_evac", half, jc - 2)])
            for hc in range(KCH):
                ist = t.matmul(pAB[par][:, :],
                               wo_buf[par][:, hc * PART:(hc + 1) * PART],
                               aoh[:, hc * 512:(hc + 1) * 512],
                               start=(hc == 0), stop=(hc == KCH - 1))
            R[("wo_mm", half, jc)] = s_pe.inc(ist)
            R[("wo_used", idx)] = s_pe.v
            if jc >= 2:
                _stats_mm(t, R, tag, jc - 2, zbf_t, sq_t, ones_col_b, pbC, pbD,
                          s_ac, s_pe)

        @block.vector
        def _(v, jc=jc, par=par):
            v.wait_ge(s_pe.sem, R[("wo_mm", half, jc)])
            if jc == 0:
                v.wait_ge(s_sld.sem, R[("p4_loads", half)])
            ist = v.scalar_tensor_tensor(z_sb[:, jc * 512:(jc + 1) * 512],
                                         pAB[par][:, :], bo_sb[:, jc:jc + 1],
                                         z_sb[:, jc * 512:(jc + 1) * 512],
                                         ALU.add, ALU.add)
            R[("z_evac", half, jc)] = s_dv.inc(ist)

        @block.scalar
        def _(sc, jc=jc):
            tpar = jc % 2
            sc.wait_ge(s_dv.sem, R[("z_evac", half, jc)])
            if jc >= 2:
                sc.wait_ge(s_pe.sem, R[("stat_mm", tag, jc - 2)])
            ist = sc.activation(zbf_t[:, tpar * 512:(tpar + 1) * 512],
                                z_sb[:, jc * 512:(jc + 1) * 512], AF.Identity)
            s_ac.inc(ist)
            ist = sc.activation(sq_t[:, tpar * 512:(tpar + 1) * 512],
                                z_sb[:, jc * 512:(jc + 1) * 512], AF.Square)
            R[("zbfsq", tag, jc)] = s_ac.inc(ist)

    @block.tensor
    def _(t):
        _stats_mm(t, R, tag, KCH - 2, zbf_t, sq_t, ones_col_b, pbC, pbD, s_ac, s_pe)
        _stats_mm(t, R, tag, KCH - 1, zbf_t, sq_t, ones_col_b, pbC, pbD, s_ac, s_pe)
        R[("wo_mm_done", half)] = s_pe.v

    _ln_finish(block, R, tag, z_sb, mean_sb, msq_sb, tmp_row, ln_row, rstd_row,
               meanbc, rstdbc, epst, ones_row_f, wide0, pbC, pbD, s_pe, s_ac, s_dv,
               bc_mean=(wide0[:, 0:512] if half == 0 else pAB[0][:, :]),
               bc_rstd=(wide0[:, 512:1024] if half == 0 else pAB[1][:, :]))

    @block.scalar
    def _(sc):
        for jc in range(KCH):
            sc.wait_ge(s_dv.sem, R[("xnorm", tag, jc)])
            ist = sc.activation(
                x1bf[:, jc * TC + half * 512: jc * TC + (half + 1) * 512],
                z_sb[:, jc * 512:(jc + 1) * 512], AF.Identity)
            s_ac.inc(ist)
        R[("x1bf_done", half)] = s_ac.v

    @block.sync
    def _(s):
        s.wait_ge(s_dv.sem, R[("ln_done", tag)])
        ist = s.dma_start(out=x1_dram[:, half * 512:(half + 1) * 512]
                          .rearrange("(c p) t -> p c t", p=PART),
                          in_=z_sb.rearrange("p (c t) -> p c t", t=512))
        s_x1[half].inc(ist, 16)
        R[("x1_store", half)] = s_x1[half].v


def _stats_mm(t, R, tag, jc, zbf_t, sq_t, ones_col_b, pbC, pbD, s_ac, s_pe):
    tpar = jc % 2
    t.wait_ge(s_ac.sem, R[("zbfsq", tag, jc)])
    t.matmul(pbC[0:1, :], ones_col_b[:, :], zbf_t[:, tpar * 512:(tpar + 1) * 512],
             start=(jc == 0), stop=(jc == KCH - 1))
    ist = t.matmul(pbD[0:1, :], ones_col_b[:, :], sq_t[:, tpar * 512:(tpar + 1) * 512],
                   start=(jc == 0), stop=(jc == KCH - 1))
    R[("stat_mm", tag, jc)] = s_pe.inc(ist)


def _ln_finish(block, R, tag, zbuf, mean_sb, msq_sb, tmp_row, ln_row, rstd_row,
               meanbc, rstdbc, epst, ones_row_f, widebc, pbC, pbD,
               s_pe, s_ac, s_dv, slice_fn=None, stat_mean=None, stat_sq=None,
               bc_mean=None, bc_rstd=None):
    if slice_fn is None:
        slice_fn = lambda jc: zbuf[:, jc * 512:(jc + 1) * 512]
    if stat_mean is None:
        stat_mean = pbC[0:1, :]
    if stat_sq is None:
        stat_sq = pbD[0:1, :]
    if bc_mean is None:
        bc_mean = widebc[:, 0:512]
    if bc_rstd is None:
        bc_rstd = widebc[:, 512:1024]

    @block.vector
    def _(v):
        v.wait_ge(s_pe.sem, R[("stat_mm", tag, KCH - 1)])
        s_dv.inc(v.tensor_scalar(mean_sb[0:1, :], stat_mean, 1.0 / D, None, ALU.mult))
        s_dv.inc(v.tensor_scalar(msq_sb[0:1, :], stat_sq, 1.0 / D, None, ALU.mult))
        R[("rows_done", tag)] = s_dv.v
        v.wait_ge(s_dv.sem, s_dv.v)
        s_dv.inc(v.tensor_mul(tmp_row[0:1, :], mean_sb[0:1, :], mean_sb[0:1, :]))
        v.wait_ge(s_dv.sem, s_dv.v)
        ist = v.tensor_sub(tmp_row[0:1, :], msq_sb[0:1, :], tmp_row[0:1, :])
        R[("var", tag)] = s_dv.inc(ist)

    @block.scalar
    def _(sc):
        sc.wait_ge(s_dv.sem, R[("var", tag)])
        s_ac.inc(sc.activation(ln_row[0:1, :], tmp_row[0:1, :], AF.Ln,
                               bias=epst[0:1, 0:1]))
        sc.wait_ge(s_ac.sem, s_ac.v)
        ist = sc.activation(rstd_row[0:1, :], ln_row[0:1, :], AF.Exp, scale=-0.5)
        R[("rstd", tag)] = s_ac.inc(ist)

    @block.tensor
    def _(t):
        t.wait_ge(s_ac.sem, R[("rstd", tag)])
        kx = R.get(("bc_extra_dv", tag))
        if kx is not None:
            t.wait_ge(s_dv.sem, kx)
        t.matmul(bc_mean, ones_row_f[0:1, :], mean_sb[0:1, :],
                 start=True, stop=True)
        ist = t.matmul(bc_rstd, ones_row_f[0:1, :], rstd_row[0:1, :],
                       start=True, stop=True)
        R[("bcast", tag)] = s_pe.inc(ist)

    @block.vector
    def _(v):
        v.wait_ge(s_pe.sem, R[("bcast", tag)])
        s_dv.inc(v.tensor_copy(meanbc[:, :], bc_mean))
        s_dv.inc(v.tensor_copy(rstdbc[:, :], bc_rstd))
        v.wait_ge(s_dv.sem, s_dv.v)
        for jc in range(KCH):
            sl = slice_fn(jc)
            s_dv.inc(v.tensor_sub(sl, sl, meanbc[:, :]))
            v.wait_ge(s_dv.sem, s_dv.v)
            ist = v.tensor_mul(sl, sl, rstdbc[:, :])
            R[("xnorm", tag, jc)] = s_dv.inc(ist)
        R[("ln_done", tag)] = s_dv.v


def _emit_p5a(block, R, x1bf, wg_buf, wu_buf, sg_t, h_t, A_t,
              pM01, pm01, pM23, pm23, thr_t, mask_t, hsp_t,
              wide0, wide1, wgate, wup, h_dram,
              s_wgu, s_hs, s_x1, s_pe, s_ac, s_dv):
    NJC = INTER // 512
    NTB8 = TC // PART

    for jc in range(NJC):
        par = jc % 2

        @block.gpsimd
        def _(g, jc=jc, par=par):
            if jc < 2:
                g.wait_ge(s_pe.sem, R[("wo_mm_done", 1)])
                g.wait_ge(s_x1[0].sem, R[("x1_store", 0)])
                g.wait_ge(s_ac.sem, R[("x1bf_done", 0)])
            else:
                g.wait_ge(s_pe.sem, R[("gu_used", jc - 2)])
            ist = g.dma_start(out=wg_buf[par].rearrange("p (c j) -> p c j", j=512),
                              in_=wgate[:, jc * 512:(jc + 1) * 512]
                                  .rearrange("(c p) j -> p c j", p=PART))
            s_wgu[par].inc(ist, 16)
            ist = g.dma_start(out=wu_buf[par].rearrange("p (c j) -> p c j", j=512),
                              in_=wup[:, jc * 512:(jc + 1) * 512]
                                  .rearrange("(c p) j -> p c j", p=PART))
            s_wgu[par].inc(ist, 16)
            R[("gu_ld", jc)] = s_wgu[par].v

        for tb8 in range(NTB8):
            it = jc * NTB8 + tb8
            ppar = it % 2

            @block.tensor
            def _(t, jc=jc, par=par, tb8=tb8, it=it, ppar=ppar):
                if tb8 == 0:
                    t.wait_ge(s_wgu[par].sem, R[("gu_ld", jc)])
                if tb8 == 0 and jc == 0:
                    t.wait_ge(s_ac.sem, R[("x1bf_done", 0)])
                if tb8 == 4 and jc == 0:
                    t.wait_ge(s_ac.sem, R[("x1bf_done", 1)])
                if it >= 2:
                    t.wait_ge(s_ac.sem, R[("silu", it - 2)])
                    t.wait_ge(s_dv.sem, R[("hmul", it - 2)])
                for kc in range(KCH):
                    lhs = x1bf[:, kc * TC + tb8 * PART: kc * TC + (tb8 + 1) * PART]
                    t.matmul(wide0[:, ppar * 512:(ppar + 1) * 512], lhs,
                             wg_buf[par][:, kc * 512:(kc + 1) * 512],
                             start=(kc == 0), stop=(kc == KCH - 1))
                    ist = t.matmul(wide1[:, ppar * 512:(ppar + 1) * 512], lhs,
                                   wu_buf[par][:, kc * 512:(kc + 1) * 512],
                                   start=(kc == 0), stop=(kc == KCH - 1))
                R[("gu_mm", it)] = s_pe.inc(ist)
                if tb8 == NTB8 - 1:
                    R[("gu_used", jc)] = s_pe.v

            @block.scalar
            def _(sc, it=it, ppar=ppar):
                sc.wait_ge(s_pe.sem, R[("gu_mm", it)])
                if it >= 2:
                    sc.wait_ge(s_dv.sem, R[("hmul", it - 2)])
                ist = sc.activation(sg_t[:, ppar * 512:(ppar + 1) * 512],
                                    wide0[:, ppar * 512:(ppar + 1) * 512], AF.Silu)
                R[("silu", it)] = s_ac.inc(ist)

            @block.vector
            def _(v, jc=jc, tb8=tb8, it=it, ppar=ppar):
                v.wait_ge(s_ac.sem, R[("silu", it)])
                v.wait_ge(s_dv.sem, s_dv.v)
                ist = v.tensor_tensor(h_t[:, :], sg_t[:, ppar * 512:(ppar + 1) * 512],
                                      wide1[:, ppar * 512:(ppar + 1) * 512], ALU.mult)
                R[("hmul", it)] = s_dv.inc(ist)
                v.wait_ge(s_dv.sem, s_dv.v)
                s_dv.inc(v.scalar_tensor_tensor(A_t[:, :], h_t[:, :], -1.0,
                                                h_t[:, :], ALU.mult, ALU.max))
                v.wait_ge(s_dv.sem, s_dv.v)
                A4 = A_t.ap().rearrange("p (g i) -> p g i", i=4)
                s_dv.inc(v.tensor_tensor(pM01[:, :], A4[:, :, 0], A4[:, :, 1], ALU.max))
                s_dv.inc(v.tensor_tensor(pm01[:, :], A4[:, :, 0], A4[:, :, 1], ALU.min))
                s_dv.inc(v.tensor_tensor(pM23[:, :], A4[:, :, 2], A4[:, :, 3], ALU.max))
                s_dv.inc(v.tensor_tensor(pm23[:, :], A4[:, :, 2], A4[:, :, 3], ALU.min))
                v.wait_ge(s_dv.sem, s_dv.v)
                s_dv.inc(v.tensor_tensor(pM01[:, :], pM01[:, :], pM23[:, :], ALU.min))
                s_dv.inc(v.tensor_tensor(pm01[:, :], pm01[:, :], pm23[:, :], ALU.max))
                v.wait_ge(s_dv.sem, s_dv.v)
                s_dv.inc(v.tensor_tensor(thr_t[:, :], pM01[:, :], pm01[:, :], ALU.max))
                v.wait_ge(s_dv.sem, s_dv.v)
                M4 = mask_t.ap().rearrange("p (g i) -> p g i", i=4)
                for i in range(4):
                    s_dv.inc(v.tensor_tensor(M4[:, :, i], A4[:, :, i], thr_t[:, :],
                                             ALU.is_ge))
                v.wait_ge(s_dv.sem, s_dv.v)
                ks = R.get(("hsp_store", it - 2))
                if ks is not None:
                    v.wait_ge(s_hs[it % 2].sem, ks)
                ist = v.tensor_tensor(hsp_t[:, ppar * 512:(ppar + 1) * 512],
                                      h_t[:, :], mask_t[:, :], ALU.mult)
                R[("hsp", it)] = s_dv.inc(ist)

            @block.sync
            def _(s, jc=jc, tb8=tb8, it=it, ppar=ppar):
                s.wait_ge(s_dv.sem, R[("hsp", it)])
                ist = s.dma_start(out=h_dram[tb8 * PART:(tb8 + 1) * PART,
                                             jc * 512:(jc + 1) * 512],
                                  in_=hsp_t[:, ppar * 512:(ppar + 1) * 512])
                s_hs[ppar].inc(ist, 16)
                R[("hsp_store", it)] = s_hs[ppar].v

    R["h_all_stored"] = (s_hs[0].v, s_hs[1].v)
    R["gu_all_used"] = s_pe.v


def _emit_p5b(block, R, hfm_q, wd_buf, z2_sb,
              zbf_t, sq_t, mean_sb, msq_sb, tmp_row, ln_row, rstd_row,
              meanbc, rstdbc, epst, ones_col_b, ones_row_f,
              wide0, wide1, pAB, pbC, pbD, wd_bf, h_dram, x1_dram, wbf_v, s_gld,
              s_wd, s_hT, s_sld, s_z2i, s_hs, s_x1, s_pe, s_ac, s_dv):
    _DOWN_PSUM = [pAB[0], pAB[1], wide0[:, 0:512], wide0[:, 512:1024]]

    @block.gpsimd
    def _(g):
        g.wait_ge(s_pe.sem, R["gu_all_used"])
        g.wait_ge(s_x1[0].sem, R[("x1_store", 0)])
        g.wait_ge(s_x1[1].sem, R[("x1_store", 1)])
        ist = g.dma_start(out=z2_sb.rearrange("p (c t) -> p c t", t=TC),
                          in_=x1_dram.rearrange("(c p) t -> p c t", p=PART))
        s_z2i.inc(ist, 16)
        R["z2_init"] = s_z2i.v

    for ipart in range(4):
        hpar = ipart % 2

        @block.sync
        def _(s, ipart=ipart, hpar=hpar):
            if ipart == 0:
                hs0, hs1 = R["h_all_stored"]
                if hs0:
                    s.wait_ge(s_hs[0].sem, hs0)
                if hs1:
                    s.wait_ge(s_hs[1].sem, hs1)
            if ipart >= 2:
                s.wait_ge(s_pe.sem, R[("down_pe", ipart - 2)])
            for i16 in range(16):
                ic = ipart * 16 + i16
                ist = s.dma_start_transpose(
                    hfm_q[hpar][:, i16 * 1024:(i16 + 1) * 1024],
                    h_dram[:, ic * PART:(ic + 1) * PART])
                s_hT[hpar].inc(ist, 16)
            R[("hT", ipart)] = s_hT[hpar].v

        for j2 in range(KCH):
            wpar = j2 % 2
            idx = ipart * KCH + j2

            @block.gpsimd
            def _(g, ipart=ipart, j2=j2, wpar=wpar, idx=idx):
                if idx >= 2:
                    g.wait_ge(s_pe.sem, R[("wd_used", idx - 2)])
                elif idx == 0:
                    g.wait_ge(s_pe.sem, R["gu_all_used"])
                    g.wait_ge(s_gld.sem, wbf_v)
                ist = g.dma_start(
                    out=wd_buf[wpar].rearrange("p (c j) -> p c j", j=PART),
                    in_=wd_bf[ipart * 2048:(ipart + 1) * 2048,
                              j2 * PART:(j2 + 1) * PART]
                        .rearrange("(c p) j -> p c j", p=PART))
                s_wd[wpar].inc(ist, 16)
                R[("wd_ld", idx)] = s_wd[wpar].v

            @block.tensor
            def _(t, ipart=ipart, j2=j2, wpar=wpar, idx=idx, hpar=hpar):
                if j2 == 0:
                    t.wait_ge(s_hT[hpar].sem, R[("hT", ipart)])
                t.wait_ge(s_wd[wpar].sem, R[("wd_ld", idx)])
                for th in range(2):
                    pos = j2 * 2 + th
                    dtile = _DOWN_PSUM[pos % 4]
                    kz = R.get(("z2_add", ipart, pos - 4))
                    if kz is None and ipart >= 1:
                        kz = R.get(("z2_add", ipart - 1, 28 + pos))
                    if kz is not None:
                        t.wait_ge(s_dv.sem, kz)
                    for c in range(16):
                        ist = t.matmul(dtile[:, :],
                                       wd_buf[wpar][:, c * PART:(c + 1) * PART],
                                       hfm_q[hpar][:, c * 1024 + th * 512:
                                                   c * 1024 + (th + 1) * 512],
                                       start=(c == 0), stop=(c == 15))
                    R[("down_mm", ipart, pos)] = s_pe.inc(ist)
                R[("wd_used", idx)] = s_pe.v

            @block.vector
            def _(v, ipart=ipart, j2=j2):
                if ipart == 0 and j2 == 0:
                    v.wait_ge(s_z2i.sem, R["z2_init"])
                for th in range(2):
                    pos = j2 * 2 + th
                    dtile = _DOWN_PSUM[pos % 4]
                    v.wait_ge(s_pe.sem, R[("down_mm", ipart, pos)])
                    sl = z2_sb[:, j2 * TC + th * 512: j2 * TC + (th + 1) * 512]
                    ist = v.tensor_add(sl, sl, dtile[:, :])
                    R[("z2_add", ipart, pos)] = s_dv.inc(ist)

            if ipart == 3:
                @block.scalar
                def _(sc, j2=j2):
                    for th in range(2):
                        tag = ("ln2", th)
                        sc.wait_ge(s_dv.sem, R[("z2_add", 3, j2 * 2 + th)])
                        if j2 >= 1:
                            sc.wait_ge(s_pe.sem, R[("stat_mm", tag, j2 - 1)])
                        sl = z2_sb[:, j2 * TC + th * 512: j2 * TC + (th + 1) * 512]
                        ist = sc.activation(zbf_t[:, th * 512:(th + 1) * 512],
                                            sl, AF.Identity)
                        s_ac.inc(ist)
                        ist = sc.activation(sq_t[:, th * 512:(th + 1) * 512],
                                            sl, AF.Square)
                        R[("zbfsq", tag, j2)] = s_ac.inc(ist)

                @block.tensor
                def _(t, j2=j2):
                    for th in range(2):
                        tag = ("ln2", th)
                        mrow = pbC[0:1, :] if th == 0 else wide1[0:1, 0:512]
                        qrow = pbD[0:1, :] if th == 0 else wide1[0:1, 512:1024]
                        t.wait_ge(s_ac.sem, R[("zbfsq", tag, j2)])
                        t.matmul(mrow, ones_col_b[:, :],
                                 zbf_t[:, th * 512:(th + 1) * 512],
                                 start=(j2 == 0), stop=(j2 == KCH - 1))
                        ist = t.matmul(qrow, ones_col_b[:, :],
                                       sq_t[:, th * 512:(th + 1) * 512],
                                       start=(j2 == 0), stop=(j2 == KCH - 1))
                        R[("stat_mm", tag, j2)] = s_pe.inc(ist)

        @block.tensor
        def _(t, ipart=ipart):
            R[("down_pe", ipart)] = s_pe.v

    R[("bc_extra_dv", ("ln2", 0))] = R[("z2_add", 3, 31)]
    for th in range(2):
        _ln_finish(block, R, ("ln2", th), None,
                   mean_sb, msq_sb, tmp_row, ln_row, rstd_row,
                   meanbc, rstdbc, epst, ones_row_f, wide0, pbC, pbD,
                   s_pe, s_ac, s_dv,
                   slice_fn=lambda jc, th=th: z2_sb[:, jc * TC + th * 512:
                                                    jc * TC + (th + 1) * 512],
                   stat_mean=(pbC[0:1, :] if th == 0 else wide1[0:1, 0:512]),
                   stat_sq=(pbD[0:1, :] if th == 0 else wide1[0:1, 512:1024]),
                   bc_mean=(wide0[:, 0:512] if th == 0 else pbC[:, :]),
                   bc_rstd=(wide0[:, 512:1024] if th == 0 else pbD[:, :]))
        R[("ln2_done", th)] = R[("ln_done", ("ln2", th))]


def _emit_p6_half(block, R, half, z2_sb, panel, identf, wide1, y_out,
                  s_pe, s_dv, s_ot):
    for t128 in range(4):
        pidx = half * 4 + t128
        pi = pidx % 2

        for jc in range(KCH):
            ti = jc % 2

            @block.tensor
            def _(t, t128=t128, jc=jc, ti=ti):
                t.wait_ge(s_dv.sem, R[("xnorm", ("ln2", half), jc)])
                if t128 == 0 and jc == 0 and half == 0:
                    t.wait_ge(s_dv.sem, R[("rows_done", ("ln2", 1))])
                idx = t128 * KCH + jc
                ko = R.get(("outt_copied", half, idx - 2))
                if ko is None and half == 1:
                    ko = R.get(("outt_copied", 0, 4 * KCH + idx - 2))
                if ko is not None:
                    t.wait_ge(s_dv.sem, ko)
                ist = t.transpose(
                    wide1[:, ti * 512: ti * 512 + 128],
                    z2_sb[:, jc * TC + half * 512 + t128 * PART:
                          jc * TC + half * 512 + (t128 + 1) * PART],
                    identf[:, :])
                R[("tr", half, t128, jc)] = s_pe.inc(ist)

            @block.vector
            def _(v, t128=t128, jc=jc, ti=ti, pi=pi, pidx=pidx):
                v.wait_ge(s_pe.sem, R[("tr", half, t128, jc)])
                if jc == 0:
                    ks = R.get(("panel_stored", pidx - 2))
                    if ks is not None:
                        v.wait_ge(s_ot[pi].sem, ks)
                idx = t128 * KCH + jc
                ist = v.tensor_copy(panel[pi][:, jc * PART:(jc + 1) * PART],
                                    wide1[:, ti * 512: ti * 512 + 128])
                R[("outt_copied", half, idx)] = s_dv.inc(ist)

        @block.sync
        def _(s, t128=t128, pi=pi, pidx=pidx):
            s.wait_ge(s_dv.sem, R[("outt_copied", half, t128 * KCH + KCH - 1)])
            trow = half * 512 + t128 * PART
            ist = s.dma_start(out=y_out[trow:trow + PART, :], in_=panel[pi][:, :])
            s_ot[pi].inc(ist, 16)
            R[("panel_stored", pidx)] = s_ot[pi].v
    R[("out_done", half)] = (s_ot[0].v, s_ot[1].v)



_NC_CACHE = {}


def _get_nc():
    if "full" not in _NC_CACHE:
        _NC_CACHE["full"] = build_nc()
    return _NC_CACHE["full"]


def make_in_maps(L0_attn_x, L0_attn_Wqkv, L0_attn_bqkv, L0_attn_Wo, L0_attn_bo,
                 L0_ffn_w_gate, L0_ffn_w_up, L0_ffn_w_down):
    x = np.ascontiguousarray(np.asarray(L0_attn_x, dtype=np.float32).reshape(S, D).T)
    Wqkv = np.asarray(L0_attn_Wqkv, dtype=np.float32)
    bqkv = np.asarray(L0_attn_bqkv, dtype=np.float32)
    Wo = np.ascontiguousarray(np.asarray(L0_attn_Wo, dtype=np.float32))
    bo = np.ascontiguousarray(np.asarray(L0_attn_bo, dtype=np.float32))
    wg = np.ascontiguousarray(np.asarray(L0_ffn_w_gate, dtype=np.float32))
    wu = np.ascontiguousarray(np.asarray(L0_ffn_w_up, dtype=np.float32))
    wd = np.ascontiguousarray(np.asarray(L0_ffn_w_down, dtype=np.float32))
    in_maps = []
    for c in range(NCORE):
        cs = 256 * c
        in_maps.append({
            "xT": x,
            "wq": np.ascontiguousarray(Wqkv[:, cs:cs + 256]),
            "wk": np.ascontiguousarray(Wqkv[:, 2048 + cs:2048 + cs + 256]),
            "wv": np.ascontiguousarray(Wqkv[:, 4096 + cs:4096 + cs + 256]),
            "bq": np.ascontiguousarray(bqkv[cs:cs + 256]),
            "bk": np.ascontiguousarray(bqkv[2048 + cs:2048 + cs + 256]),
            "bv": np.ascontiguousarray(bqkv[4096 + cs:4096 + cs + 256]),
            "wo": Wo,
            "bo": bo,
            "xres": np.ascontiguousarray(x[:, TC * c:TC * (c + 1)]),
            "wgate": wg,
            "wup": wu,
            "wdown": wd,
        })
    return in_maps


def kernel(**inputs):
    in_maps = make_in_maps(**inputs)
    nc = _get_nc()
    res = run_bass_kernel_spmd(nc, in_maps, core_ids=list(range(NCORE)))
    y = np.concatenate([np.asarray(res.results[c]["y"], dtype=np.float32)
                        for c in range(NCORE)], axis=0)
    return y.reshape(1, S, D)
